# version 31 (final)
# baseline (speedup 1.0000x reference)
import numpy as np
from contextlib import ExitStack

import concourse.bass as bass
import concourse.mybir as mybir
from concourse.bass import IndirectOffsetOnAxis
from concourse.bass_utils import run_bass_kernel_spmd

F32 = mybir.dt.float32
BF16 = mybir.dt.bfloat16
I32 = mybir.dt.int32
AF = mybir.ActivationFunctionType
ALU = mybir.AluOpType
AX = mybir.AxisListType

B, T, D = 4, 4096, 1024
NH, HD = 16, 64
TC = 2048
NCHUNK = D // 128
NSLAB = T // 128
NG = 4
E, FF = 8, 2048
CAP = (B * T) // E
CAPC = 512
NITER = 20

ENGINES = ("sync", "gpsimd", "tensor", "vector", "scalar")


class Sem:
    def __init__(self, handle, name, dma=False):
        self.h = handle
        self.name = name
        self.v = 0
        self.dma = dma


class Res:
    __slots__ = ("name", "prod", "cons")

    def __init__(self, name):
        self.name = name
        self.prod = None
        self.cons = []


class Planner:
    def __init__(self, nc, sems):
        self.nc = nc
        self.streams = {e: [] for e in ENGINES}
        self.sems = sems
        self.floor = {e: {} for e in ENGINES}
        self.pending = {e: [] for e in ENGINES}
        self.last_written = {e: [set(), set(), set()] for e in ENGINES}
        self.spacer_aps = {}

    def _wait(self, eng, sem, val):
        if sem.dma:
            val = sem.v
        if val <= 0:
            return
        fl = self.floor[eng]
        if fl.get(sem.name, 0) >= val:
            return
        fl[sem.name] = val
        h = sem.h
        self.streams[eng].append(("wait", h, val))

    def emit(self, eng, meth, *args, reads=(), writes=(), sem=None, inc=0,
             mark=True, tag=None, **kw):
        if eng in ("vector", "scalar"):
            hist = self.last_written[eng]
            lw = hist[0] | hist[1] | hist[2]
            if any((r is w) for r in reads for w in lw):
                self._emit_spacer(eng)
        waits = []
        for r in reads:
            if r.prod is not None:
                waits.append(r.prod)
        for r in writes:
            if r.prod is not None:
                waits.append(r.prod)
            waits.extend(r.cons)
        bysem = {}
        for (s, v) in waits:
            if bysem.get(s, 0) < v:
                bysem[s] = v
        for s2, v2 in bysem.items():
            self._wait(eng, s2, v2)

        if sem is None and mark:
            sem = self.sems[eng]
            inc = 1
        if sem is not None:
            sem.v += inc
            markrec = (sem, sem.v)
            if eng in ("tensor", "vector", "scalar") and sem is self.sems[eng]:
                self.floor[eng][sem.name] = sem.v
        else:
            markrec = None

        self.streams[eng].append(("op", meth, args, kw, sem.h if sem else None, inc, tag))

        recs = [(r, "r") for r in reads] + [(r, "w") for r in writes]
        if markrec is None:
            self.pending[eng].extend(recs)
        else:
            for r, kind in self.pending[eng]:
                self._apply(r, kind, markrec)
            self.pending[eng] = []
            for r, kind in recs:
                self._apply(r, kind, markrec)

        if eng in ("vector", "scalar"):
            hist = self.last_written[eng]
            self.last_written[eng] = [set(writes), hist[0], hist[1]]

    def _emit_spacer(self, eng):
        (o1, i1), (o2, i2) = self.spacer_aps[eng]
        meth = "tensor_copy" if eng == "vector" else "copy"
        self.streams[eng].append(("op", meth, (), {"out": o1, "in_": i1}, None, 0, "spacer"))
        self.streams[eng].append(("op", meth, (), {"out": o2, "in_": i2}, None, 0, "spacer"))
        self.last_written[eng] = [set(), set(), set()]

    @staticmethod
    def _apply(r, kind, markrec):
        if kind == "r":
            r.cons.append(markrec)
        else:
            r.prod = markrec
            r.cons = []

    def barrier_from(self, eng, other_sem, val):
        self._wait(eng, other_sem, val)

    def scope_barrier(self, sems):
        targets = [(s, s.v) for s in sems if s.v > 0]
        for eng in ENGINES:
            for s, v in targets:
                self._wait(eng, s, v)

    def play(self, block):
        def mk(eng):
            def run(handle):
                for item in self.streams[eng]:
                    if item[0] == "wait":
                        handle.wait_ge(item[1], item[2])
                    else:
                        _, meth, args, kw, semh, inc = item[:6]
                        ins = getattr(handle, meth)(*args, **kw)
                        if semh is not None:
                            ins.then_inc(semh, inc)
            return run
        block.sync(mk("sync"))
        block.gpsimd(mk("gpsimd"))
        block.tensor(mk("tensor"))
        block.vector(mk("vector"))
        block.scalar(mk("scalar"))

def build_nc(debug=False):
    nc = bass.Bass("TRN2", target_bir_lowering=False)

    xb = nc.dram_tensor("xb", [T, D], F32, kind="ExternalInput")
    wqT = nc.dram_tensor("wqT", [D, D], F32, kind="ExternalInput")
    wkT = nc.dram_tensor("wkT", [D, HD], F32, kind="ExternalInput")
    wvT = nc.dram_tensor("wvT", [D, HD], F32, kind="ExternalInput")
    woT = nc.dram_tensor("woT", [D, D], F32, kind="ExternalInput")
    bq2 = nc.dram_tensor("bq2", [128, NCHUNK], F32, kind="ExternalInput")
    bo2 = nc.dram_tensor("bo2", [128, NCHUNK], F32, kind="ExternalInput")
    bvr = nc.dram_tensor("bvr", [1, HD], F32, kind="ExternalInput")
    wg = nc.dram_tensor("wg", [D, E], F32, kind="ExternalInput")
    w1 = nc.dram_tensor("w1", [E, D, FF], F32, kind="ExternalInput")
    w2 = nc.dram_tensor("w2", [E, FF, D], F32, kind="ExternalInput")
    identf = nc.dram_tensor("identf", [128, 128], F32, kind="ExternalInput")
    onesf = nc.dram_tensor("onesf", [128, 128], F32, kind="ExternalInput")
    selmat = nc.dram_tensor("selmat", [E, 128, 128], F32, kind="ExternalInput")
    yout = nc.dram_tensor("yout", [TC, D], F32, kind="ExternalOutput")

    dbg = {}
    if debug:
        for name, shp in [("d_S0", [128, NCHUNK, TC]), ("d_S1", [128, NCHUNK, TC]),
                          ("d_logit", [E, TC]), ("d_pp", [128, E, 128]),
                          ("d_lo", [128, E]), ("d_own", [128, 128]),
                          ("d_gate", [128, E, 128]), ("d_yacc", [128, 16, D]),
                          ("d_ag2", [NCORES_AG * E, TC])]:
            dbg[name] = nc.dram_tensor(name, shp, F32, kind="ExternalOutput")

    xsort = nc.dram_tensor("xsort", [E * CAPC, D], F32)
    ysort = nc.dram_tensor("ysort", [E * CAPC, D], F32)
    lmask = nc.dram_tensor("lmask", [128, 128], F32, kind="ExternalInput")
    ag_in = nc.dram_tensor("ag_in", [E, TC], F32)
    ag_out = nc.dram_tensor("ag_out", [NCORES_AG * E, TC], F32, addr_space="Shared")
    ag2_in = nc.dram_tensor("ag2_in", [1, 16], F32)
    ag2_out = nc.dram_tensor("ag2_out", [NCORES_AG, 16], F32, addr_space="Shared")

    ctx = ExitStack()
    EE = ctx.enter_context

    s_in = Sem(EE(nc.semaphore("s_in")), "s_in", dma=True)
    s_w = Sem(EE(nc.semaphore("s_w")), "s_w", dma=True)
    s_w1 = Sem(EE(nc.semaphore("s_w1")), "s_w1", dma=True)
    s_w2 = Sem(EE(nc.semaphore("s_w2")), "s_w2", dma=True)
    s_x = [Sem(EE(nc.semaphore(f"s_x{i}")), f"s_x{i}", dma=True) for i in range(3)]
    s_xe = Sem(EE(nc.semaphore("s_xe")), "s_xe", dma=True)
    s_ys = Sem(EE(nc.semaphore("s_ys")), "s_ys", dma=True)
    s_pe = Sem(EE(nc.semaphore("s_pe")), "s_pe")
    s_act = Sem(EE(nc.semaphore("s_act")), "s_act")
    s_dve = Sem(EE(nc.semaphore("s_dve")), "s_dve")
    s_cc = Sem(EE(nc.semaphore("s_cc")), "s_cc", dma=True)

    P = Planner(nc, {"tensor": s_pe, "scalar": s_act, "vector": s_dve,
                     "sync": s_in, "gpsimd": s_w})

    def R(name):
        return Res(name)

    spc_a = EE(nc.sbuf_tensor("spc_a", [1, 256], F32))
    spc_b = EE(nc.sbuf_tensor("spc_b", [1, 256], F32))

    ps2 = [EE(nc.psum_tensor(f"ps2_{i}", [128, 1024], F32)) for i in range(3)]
    pr2 = [R(f"ps2_{i}") for i in range(3)]
    pbX = EE(nc.psum_tensor("pbX", [128, 512], F32)); prX = R("pbX")
    pbY = EE(nc.psum_tensor("pbY", [128, 512], F32)); prY = R("pbY")

    bq2s_t = EE(nc.sbuf_tensor("bq2s", [128, NCHUNK], F32)); bq2s_r = R("bq2s")
    bo2_t = EE(nc.sbuf_tensor("bo2t", [128, NCHUNK], F32)); bo2_r = R("bo2t")
    bvb_t = EE(nc.sbuf_tensor("bvb", [128, HD], F32)); bvb_r = R("bvb")
    bvr_t = EE(nc.sbuf_tensor("bvrt", [1, HD], F32)); bvr_r = R("bvrt")
    idf_t = EE(nc.sbuf_tensor("idf", [128, 128], F32)); idf_r = R("idf")
    ones_t = EE(nc.sbuf_tensor("ones", [128, 128], F32)); ones_r = R("ones")
    P.spacer_aps["vector"] = ((spc_a[0:1, 0:64], spc_a[0:1, 128:192]),
                              (spc_a[0:1, 64:128], spc_a[0:1, 192:256]))
    P.spacer_aps["scalar"] = ((spc_b[0:1, 0:64], spc_b[0:1, 128:192]),
                              (spc_b[0:1, 64:128], spc_b[0:1, 192:256]))
    wgf_t = EE(nc.sbuf_tensor("wgf", [128, NCHUNK, E], F32)); wgf_r = R("wgf")
    selm_t = EE(nc.sbuf_tensor("selm", [128, E, 128], F32)); selm_r = R("selm")
    owng_t = EE(nc.sbuf_tensor("owng", [128, 128], F32)); owng_r = R("owng")
    eps_t = EE(nc.sbuf_tensor("epsT", [128, 1], F32)); eps_r = R("epsT")
    wgs_t = EE(nc.sbuf_tensor("wgs", [E, 1], F32)); wgs_r = R("wgs")
    lmask_t = EE(nc.sbuf_tensor("lmask_t", [128, 128], F32)); lmask_r = R("lmask_t")
    ones16_t = EE(nc.sbuf_tensor("ones16", [128, 64], BF16)); ones16_r = R("ones16")
    rowsb_t = EE(nc.sbuf_tensor("rowsb", [1, 1024], BF16)); rowsb_r = R("rowsb")
    gate_tm = EE(nc.sbuf_tensor("gate_tm", [128, 16], F32)); gate_tm_r = R("gate_tm")
    slot_i = EE(nc.sbuf_tensor("slot_i", [128, 16], I32)); slot_r = R("slot_i")

    stackS = ExitStack()
    S_t = stackS.enter_context(nc.sbuf_tensor("S", [128, NCHUNK, TC], F32, side="right"))
    S_r = R("S")

    ge = P.emit
    ge("sync", "dma_start", out=bq2s_t[:], in_=bq2[:], reads=(), writes=(bq2s_r,), sem=s_in, inc=16)
    ge("sync", "dma_start", out=bo2_t[:], in_=bo2[:], writes=(bo2_r,), sem=s_in, inc=16)
    ge("sync", "dma_start", out=bvr_t[:], in_=bvr[:], writes=(bvr_r,), sem=s_in, inc=16)
    ge("sync", "dma_start", out=idf_t[:], in_=identf[:], writes=(idf_r,), sem=s_in, inc=16)
    ge("sync", "dma_start", out=ones_t[:], in_=onesf[:], writes=(ones_r,), sem=s_in, inc=16)
    ge("sync", "dma_start", out=wgf_t[:],
       in_=wg[:, :].rearrange("(c p) e -> p c e", p=128), writes=(wgf_r,), sem=s_in, inc=16)
    ge("sync", "dma_start", out=selm_t[:], in_=selmat[:, :, :].rearrange("e p f -> p e f"),
       writes=(selm_r,), sem=s_in, inc=16)
    ge("sync", "dma_start", out=lmask_t[:], in_=lmask[:], writes=(lmask_r,), sem=s_in, inc=16)

    ge("vector", "memset", spc_a[:], 0.0)
    ge("scalar", "memzero", spc_b[:])
    ge("vector", "memset", eps_t[:], 1e-5, writes=(eps_r,))
    ge("vector", "tensor_copy", out=ones16_t[:], in_=ones_t[:, 0:64],
       reads=(ones_r,), writes=(ones16_r,))
    ge("vector", "tensor_single_scalar", out=bq2s_t[:], in_=bq2s_t[:], scalar=0.125,
       op=ALU.mult, reads=(bq2s_r,), writes=(bq2s_r,))
    for ci in range(NCHUNK):
        ge("tensor", "matmul", out=pbY[0:E, 0:1], lhsT=wgf_t[:, ci, :], rhs=ones_t[:, 0:1],
           start=(ci == 0), stop=(ci == NCHUNK - 1), reads=(ones_r, wgf_r),
           writes=(prY,), mark=(ci == NCHUNK - 1))
    ge("vector", "tensor_copy", out=wgs_t[:], in_=pbY[0:E, 0:1], reads=(prY,), writes=(wgs_r,))
    ge("tensor", "matmul", out=pbX[:, 0:HD], lhsT=ones_t[0:1, :], rhs=bvr_t[:],
       start=True, stop=True, reads=(ones_r, bvr_r), writes=(prX,))
    ge("vector", "tensor_copy", out=bvb_t[:], in_=pbX[:, 0:HD],
       reads=(prX,), writes=(bvb_r,))

    scopeA = ExitStack()
    EA = scopeA.enter_context
    wq16 = EA(nc.sbuf_tensor("wq16", [128, NCHUNK, D], BF16)); wq16_r = R("wq16")
    wo16 = EA(nc.sbuf_tensor("wo16", [128, NCHUNK, D], BF16)); wo16_r = R("wo16")
    wk16 = EA(nc.sbuf_tensor("wk16", [128, NCHUNK, HD], BF16)); wk16_r = R("wk16")
    wv16 = EA(nc.sbuf_tensor("wv16", [128, NCHUNK, HD], BF16)); wv16_r = R("wv16")
    xslab = [EA(nc.sbuf_tensor(f"xslab{i}", [128, D], F32)) for i in range(3)]
    xslab_r = [R(f"xslab{i}") for i in range(3)]
    xfm16 = [EA(nc.sbuf_tensor(f"xfm16_{i}", [128, NCHUNK, 128], BF16)) for i in range(2)]
    xfm16_r = [R(f"xfm16_{i}") for i in range(2)]
    qg16 = EA(nc.sbuf_tensor("qg16", [128, NH // 2, 512], BF16)); qg16_r = R("qg16")
    Pb2 = [EA(nc.sbuf_tensor(f"Pb2_{i}", [128, 1024], BF16)) for i in range(3)]
    Pb2_r = [R(f"Pb2_{i}") for i in range(3)]
    ctx16 = EA(nc.sbuf_tensor("ctx16", [128, NCHUNK, 512], BF16)); ctx16_r = R("ctx16")
    x0g16, x0g16_r = ctx16, ctx16_r
    sq_sb = EA(nc.sbuf_tensor("sq_sb", [128, 512], F32)); sq_sb_r = R("sq_sb")
    rows_t = EA(nc.sbuf_tensor("rows", [1, 1024], F32)); rows_r = R("rows")
    rb = [EA(nc.sbuf_tensor(f"rb{i}", [64, 512], F32)) for i in range(2)]
    rb_r = [R(f"rb{i}") for i in range(2)]
    Kd = EA(nc.sbuf_tensor("Kd", [128, T], BF16)); Kd_r = R("Kd")
    Vt = EA(nc.sbuf_tensor("Vt", [128, NSLAB * (HD + 1)], BF16)); Vt_r = R("Vt")
    logit_sb = EA(nc.sbuf_tensor("logit_sb", [E, TC], F32)); logit_r = R("logit_sb")

    ge("gpsimd", "dma_start", out=wq16[:],
       in_=wqT[:, :].rearrange("(c p) o -> p c o", p=128), writes=(wq16_r,), sem=s_w, inc=16)
    ge("gpsimd", "dma_start", out=wk16[:],
       in_=wkT[:, :].rearrange("(c p) o -> p c o", p=128), writes=(wk16_r,), sem=s_w, inc=16)
    ge("gpsimd", "dma_start", out=wv16[:],
       in_=wvT[:, :].rearrange("(c p) o -> p c o", p=128), writes=(wv16_r,), sem=s_w, inc=16)
    ge("gpsimd", "dma_start", out=wo16[:],
       in_=woT[:, :].rearrange("(c p) o -> p c o", p=128), writes=(wo16_r,), sem=s_w, inc=16)

    ge("vector", "memset", Vt[:], 1.0, writes=(Vt_r,))

    for si in range(NSLAB):
        cur = si % 3
        fcur = si % 2
        jt = si % 3
        ge("sync", "dma_start", out=xslab[cur][:], in_=xb[si * 128:(si + 1) * 128, :],
           reads=(), writes=(xslab_r[cur],), sem=s_x[cur], inc=16)
        for ci in range(NCHUNK):
            ge("tensor", "transpose", out=ps2[jt][:, ci * 128:(ci + 1) * 128],
               in_=xslab[cur][:, ci * 128:(ci + 1) * 128], identity=idf_t[:],
               reads=(xslab_r[cur], idf_r), writes=(pr2[jt],), mark=(ci == NCHUNK - 1))
        ge("vector", "tensor_copy", out=xfm16[fcur][:, :, :],
           in_=ps2[jt][:].rearrange("p (c t) -> p c t", c=NCHUNK),
           reads=(pr2[jt],), writes=(xfm16_r[fcur],), mark=(si >= 16))
        if si < 16:
            ge("vector", "tensor_copy", out=S_t[:, :, si * 128:(si + 1) * 128],
               in_=ps2[jt][:].rearrange("p (c t) -> p c t", c=NCHUNK),
               reads=(pr2[jt],), writes=(S_r,))
        for ci in range(NCHUNK):
            ge("tensor", "matmul", out=pbX[0:64, 0:128], lhsT=wk16[:, ci, :],
               rhs=xfm16[fcur][:, ci, :], start=(ci == 0), stop=(ci == NCHUNK - 1),
               reads=(wk16_r, xfm16_r[fcur]), writes=(prX,), mark=(ci == NCHUNK - 1))
        ge("vector", "tensor_copy", out=Kd[0:64, si * 128:(si + 1) * 128],
           in_=pbX[0:64, 0:128], reads=(prX,), writes=(Kd_r,), mark=False)
        ge("vector", "tensor_copy", out=Kd[64:128, si * 128:(si + 1) * 128],
           in_=pbX[0:64, 0:128], reads=(prX,), writes=(Kd_r,))
        for ci in range(NCHUNK):
            ge("tensor", "matmul", out=pbY[:, 0:HD], lhsT=xfm16[fcur][:, ci, :],
               rhs=wv16[:, ci, :], start=(ci == 0), stop=(ci == NCHUNK - 1),
               reads=(wv16_r, xfm16_r[fcur]), writes=(prY,), mark=(ci == NCHUNK - 1))
        ge("vector", "tensor_tensor",
           out=Vt[:, si * (HD + 1): si * (HD + 1) + HD], in0=pbY[:, 0:HD],
           in1=bvb_t[:], op=ALU.add, reads=(prY, bvb_r), writes=(Vt_r,))

    if debug:
        ge("sync", "dma_start", out=dbg["d_S0"][:], in_=S_t[:], reads=(S_r,), sem=s_in, inc=16)

    GS = 512

    def emit_x0q(g):
        gsl = slice(g * GS, (g + 1) * GS)
        ge("vector", "tensor_copy", out=x0g16[:, :, :], in_=S_t[:, :, gsl],
           reads=(S_r,), writes=(x0g16_r,))
        for oc in range(NCHUNK):
            j, h = (oc % 6) // 2, (oc % 6) % 2
            dst = ps2[j][:, h * 512:(h + 1) * 512]
            for ci in range(NCHUNK):
                ge("tensor", "matmul", out=dst, lhsT=wq16[:, ci, oc * 128:(oc + 1) * 128],
                   rhs=x0g16[:, ci, :], start=(ci == 0), stop=(ci == NCHUNK - 1),
                   reads=(wq16_r, x0g16_r), writes=(pr2[j],), mark=(ci == NCHUNK - 1))
            ge("scalar", "activation", out=qg16[:, oc, :], in_=dst,
               func=AF.Identity, bias=bq2s_t[:, oc:oc + 1], scale=0.125,
               reads=(pr2[j], bq2s_r), writes=(qg16_r,))

    emit_x0q(0)
    for g in range(NG):
        gsl = slice(g * GS, (g + 1) * GS)

        ctxtgt = [(pbX, prX, pbY, prY),
                  (ps2[2][0:128, 0:512], pr2[2], ps2[2][0:128, 512:1024], pr2[2])]

        def tail_a(hp_):
            cA, rA, cB, rB = ctxtgt[hp_ % 2]
            for par, (cb, cr) in ((0, (cA, rA)), (1, (cB, rB))):
                rsl = slice(0, 512) if par == 0 else slice(512, 1024)
                ge("scalar", "activation", out=rows_t[0:1, rsl], in_=cb[HD:HD + 1, :],
                   func=AF.Ln, reads=(cr,), writes=(rows_r,), mark=False)
                ge("scalar", "activation", out=rowsb_t[0:1, rsl], in_=rows_t[0:1, rsl],
                   func=AF.Exp, scale=-1.0, reads=(rows_r,), writes=(rowsb_r,))
                ge("vector", "tensor_copy", out=rb[par][:], in_=cb[0:HD, :],
                   reads=(cr,), writes=(rb_r[par],))

        def tail_b(hp_):
            cA, rA, cB, rB = ctxtgt[hp_ % 2]
            for par, (cb, cr) in ((0, (cA, rA)), (1, (cB, rB))):
                rsl = slice(0, 512) if par == 0 else slice(512, 1024)
                ge("tensor", "matmul", out=cb[0:HD, :], lhsT=ones16_t[0:1, :],
                   rhs=rowsb_t[0:1, rsl], start=True, stop=True,
                   reads=(rowsb_r, ones16_r, rb_r[par]), writes=(cr,))

        def tail_c(hp_):
            cA, rA, cB, rB = ctxtgt[hp_ % 2]
            for par, (cb, cr) in ((0, (cA, rA)), (1, (cB, rB))):
                ge("vector", "tensor_tensor", out=ctx16[par * 64:(par + 1) * 64, hp_, :],
                   in0=rb[par][:], in1=cb[0:HD, :], op=ALU.mult,
                   reads=(cr, rb_r[par]), writes=(ctx16_r,))

        for hp in range(NH // 2):
            cA, rA, cB, rB = ctxtgt[hp % 2]
            for scs in range(NSLAB + 2):
                if scs < NSLAB:
                    sc = scs
                    j = sc % 2
                    jp = sc % 3
                    ksl = slice(sc * 128, (sc + 1) * 128)
                    ge("tensor", "matmul", out=ps2[j][:, 0:512], lhsT=Kd[0:64, ksl],
                       rhs=qg16[0:64, hp, :], start=True, stop=True, tile_position=(0, 0),
                       reads=(Kd_r, qg16_r), writes=(pr2[j],), mark=False)
                    ge("tensor", "matmul", out=ps2[j][:, 512:1024], lhsT=Kd[64:128, ksl],
                       rhs=qg16[64:128, hp, :], start=True, stop=True, tile_position=(64, 0),
                       reads=(Kd_r, qg16_r), writes=(pr2[j],))
                    ge("scalar", "activation", out=Pb2[jp][:], in_=ps2[j][:], func=AF.Exp,
                       reads=(pr2[j],), writes=(Pb2_r[jp],))
                if hp > 0 and scs == 0:
                    tail_a(hp - 1)
                if hp > 0 and scs == 1:
                    tail_b(hp - 1)
                if hp > 0 and scs == 2:
                    tail_c(hp - 1)
                if scs >= 2:
                    sc = scs - 2
                    jp = sc % 3
                    vsl = slice(sc * (HD + 1), (sc + 1) * (HD + 1))
                    ge("tensor", "matmul", out=cA[0:HD + 1, :], lhsT=Vt[:, vsl],
                       rhs=Pb2[jp][:, 0:512], start=(sc == 0), stop=(sc == NSLAB - 1),
                       skip_group_check=True, reads=(Vt_r, Pb2_r[jp]), writes=(rA,), mark=False)
                    ge("tensor", "matmul", out=cB[0:HD + 1, :], lhsT=Vt[:, vsl],
                       rhs=Pb2[jp][:, 512:1024], start=(sc == 0), stop=(sc == NSLAB - 1),
                       skip_group_check=True, reads=(Vt_r, Pb2_r[jp]), writes=(rB,))
        tail_a(NH // 2 - 1)
        tail_b(NH // 2 - 1)
        tail_c(NH // 2 - 1)
        def stats_oc(oc):
            ge("tensor", "matmul", out=pbX[0:1, :], lhsT=ones_t[:, 0:1],
               rhs=S_t[:, oc, gsl], start=(oc == 0), stop=(oc == NCHUNK - 1),
               skip_group_check=True, reads=(S_r, ones_r), writes=(prX,),
               mark=(oc == NCHUNK - 1))
            ge("scalar", "activation", out=sq_sb[:], in_=S_t[:, oc, gsl], func=AF.Square,
               reads=(S_r,), writes=(sq_sb_r,))
            ge("tensor", "matmul", out=pbY[0:1, :], lhsT=ones_t[:, 0:1],
               rhs=sq_sb[:], start=(oc == 0), stop=(oc == NCHUNK - 1),
               skip_group_check=True, reads=(sq_sb_r, ones_r), writes=(prY,))

        for oc in range(NCHUNK + 1):
            if oc < NCHUNK:
                j, h = (oc % 6) // 2, (oc % 6) % 2
                dst = ps2[j][:, h * 512:(h + 1) * 512]
                for ci in range(NCHUNK):
                    ge("tensor", "matmul", out=dst, lhsT=wo16[:, ci, oc * 128:(oc + 1) * 128],
                       rhs=ctx16[:, ci, :], start=(ci == 0), stop=(ci == NCHUNK - 1),
                       reads=(wo16_r, ctx16_r), writes=(pr2[j],), mark=(ci == NCHUNK - 1))
                ge("vector", "scalar_tensor_tensor", out=S_t[:, oc, gsl], in0=dst,
                   scalar=bo2_t[:, oc:oc + 1], in1=S_t[:, oc, gsl], op0=ALU.add, op1=ALU.add,
                   reads=(pr2[j], bo2_r, S_r), writes=(S_r,))
            if oc >= 1:
                stats_oc(oc - 1)
        mrow = rows_t[0:1, 0:512]
        erow = rows_t[0:1, 512:1024]
        vrow = sq_sb[0:1, 0:512]
        rrow = rows_t[0:1, 512:1024]
        ge("vector", "tensor_single_scalar", out=mrow, in_=pbX[0:1, :], scalar=1.0 / D,
           op=ALU.mult, reads=(prX,), writes=(rows_r,), mark=False)
        ge("vector", "tensor_single_scalar", out=erow, in_=pbY[0:1, :], scalar=1.0 / D,
           op=ALU.mult, reads=(prY,), writes=(rows_r,), mark=False)
        ge("vector", "tensor_tensor", out=vrow, in0=mrow, in1=mrow, op=ALU.mult,
           reads=(rows_r,), writes=(sq_sb_r,), mark=False)
        ge("vector", "tensor_tensor", out=vrow, in0=erow, in1=vrow, op=ALU.subtract,
           reads=(rows_r, sq_sb_r), writes=(sq_sb_r,))
        for ci in range(NCHUNK):
            ge("tensor", "matmul", out=pbX[0:E, :], lhsT=wgf_t[:, ci, :],
               rhs=S_t[:, ci, gsl], start=(ci == 0), stop=(ci == NCHUNK - 1),
               reads=(wgf_r, S_r), writes=(prX,), mark=(ci == NCHUNK - 1))
        if g + 1 < NG:
            emit_x0q(g + 1)
        ge("scalar", "activation", out=vrow, in_=vrow, func=AF.Ln, bias=eps_t[0:1, :],
           reads=(sq_sb_r, eps_r), writes=(sq_sb_r,), mark=False)
        ge("scalar", "activation", out=rrow, in_=vrow, func=AF.Exp, scale=-0.5,
           reads=(sq_sb_r,), writes=(rows_r,))
        ge("vector", "tensor_copy", out=logit_sb[:, gsl], in_=pbX[0:E, :],
           reads=(prX,), writes=(logit_r,))
        ge("tensor", "matmul", out=pbY[0:E, :], lhsT=ones_t[0:1, 0:E], rhs=mrow,
           start=True, stop=True, reads=(rows_r, ones_r), writes=(prY,))
        ge("vector", "tensor_scalar", out=sq_sb[0:E, :], in0=pbY[0:E, :],
           scalar1=wgs_t[:], scalar2=None, op0=ALU.mult,
           reads=(prY, wgs_r), writes=(sq_sb_r,))
        ge("vector", "tensor_tensor", out=logit_sb[:, gsl], in0=logit_sb[:, gsl],
           in1=sq_sb[0:E, :], op=ALU.subtract, reads=(logit_r, sq_sb_r), writes=(logit_r,))
        ge("tensor", "matmul", out=pbY[0:E, :], lhsT=ones_t[0:1, 0:E], rhs=rrow,
           start=True, stop=True, reads=(rows_r, ones_r), writes=(prY,))
        ge("vector", "tensor_tensor", out=logit_sb[:, gsl], in0=logit_sb[:, gsl],
           in1=pbY[0:E, :], op=ALU.mult, reads=(logit_r, prY), writes=(logit_r,))
        ge("tensor", "matmul", out=pbX[:], lhsT=ones_t[0:1, :], rhs=mrow,
           start=True, stop=True, reads=(rows_r, ones_r), writes=(prX,), mark=False)
        ge("tensor", "matmul", out=pbY[:], lhsT=ones_t[0:1, :], rhs=rrow,
           start=True, stop=True, reads=(rows_r, ones_r), writes=(prY,))
        for oc in range(NCHUNK):
            ge("vector", "tensor_tensor", out=sq_sb[:], in0=S_t[:, oc, gsl],
               in1=pbX[:], op=ALU.subtract, reads=(S_r, prX),
               writes=(sq_sb_r,), mark=False)
            ge("vector", "tensor_tensor", out=S_t[:, oc, gsl], in0=sq_sb[:],
               in1=pbY[:], op=ALU.mult, reads=(sq_sb_r, prY),
               writes=(S_r,), mark=(oc == NCHUNK - 1))

    if debug:
        ge("sync", "dma_start", out=dbg["d_S1"][:], in_=S_t[:], reads=(S_r,), sem=s_in, inc=16)
        ge("sync", "dma_start", out=dbg["d_logit"][:], in_=logit_sb[:], reads=(logit_r,), sem=s_in, inc=16)
    ag_in_res = R("ag_in")
    ge("gpsimd", "dma_start", out=ag_in[:], in_=logit_sb[:], reads=(logit_r,),
       writes=(ag_in_res,), sem=s_w, inc=16)

    scopeA.close()

    scopeB = ExitStack()
    EB = scopeB.enter_context
    yacc = EB(nc.sbuf_tensor("yacc", [128, 16, D], F32)); yacc_r = R("yacc")
    scrP = EB(nc.sbuf_tensor("scrP", [128, E * 128], F32)); scrP_r = R("scrP")
    statc2 = [EB(nc.sbuf_tensor(f"statc{i}", [128, 6], F32)) for i in range(2)]
    statc2_r = [R(f"statc{i}") for i in range(2)]
    scopeR = ExitStack()
    ER = scopeR.enter_context
    Lp = ER(nc.sbuf_tensor("Lp", [128, E, 128], F32)); Lp_r = R("Lp")
    pp = ER(nc.sbuf_tensor("pp", [128, E, 128], F32)); pp_r = R("pp")
    mx_t = ER(nc.sbuf_tensor("mx", [128, 128], F32)); mx_r = R("mx")
    scrA = ER(nc.sbuf_tensor("scrA", [128, 128], F32)); scrA_r = R("scrA")
    scrB = ER(nc.sbuf_tensor("scrB", [128, 128], F32)); scrB_r = R("scrB")
    best_t = ER(nc.sbuf_tensor("best", [128, 128], F32)); best_r = R("best")
    bestE_t = ER(nc.sbuf_tensor("bestE", [128, 128], F32)); bestE_r = R("bestE")
    ownB = ER(nc.sbuf_tensor("ownB", [128, 16], F32)); ownB_r = R("ownB")
    match_t = ER(nc.sbuf_tensor("match", [128, 16], F32)); match_r = R("match")
    scn_t = ER(nc.sbuf_tensor("scn", [128, 16], F32)); scn_r = R("scn")
    zz16 = ER(nc.sbuf_tensor("zz16", [128, 16], F32)); zz16_r = R("zz16")
    pref_t = ER(nc.sbuf_tensor("pref", [128, 1], F32)); pref_r = R("pref")
    slot_f = ER(nc.sbuf_tensor("slot_f", [128, 16], F32)); slot_fr = R("slot_f")
    lo_t = ER(nc.sbuf_tensor("lo", [128, E], F32)); lo_r = R("lo")
    hi_t = ER(nc.sbuf_tensor("hi", [128, E], F32)); hi_r = R("hi")
    mid_t = ER(nc.sbuf_tensor("mid", [128, E], F32)); mid_r = R("mid")
    part_t = ER(nc.sbuf_tensor("part", [128, E], F32)); part_r = R("part")
    ge_t = ER(nc.sbuf_tensor("geT", [128, E], F32)); geT_r = R("geT")
    sA8 = ER(nc.sbuf_tensor("sA8", [128, E], F32)); sA8_r = R("sA8")
    sB8 = ER(nc.sbuf_tensor("sB8", [128, E], F32)); sB8_r = R("sB8")
    sq2, sq2_r = scrP, scrP_r
    ag_out_r = R("ag_out")

    for tc in range(16):
        j = tc % 3
        for ci in range(NCHUNK):
            ge("tensor", "transpose", out=ps2[j][:, ci * 128:(ci + 1) * 128],
               in_=S_t[:, ci, tc * 128:(tc + 1) * 128], identity=idf_t[:],
               reads=(S_r, idf_r), writes=(pr2[j],), mark=(ci == NCHUNK - 1))
        ge("vector", "tensor_copy", out=yacc[:, tc, :], in_=ps2[j][:],
           reads=(pr2[j],), writes=(yacc_r,))

    sguard = Res("sguard")
    sguard.prod = (s_pe, s_pe.v)

    ge("gpsimd", "collective_compute", "AllGather", ALU.bypass,
       replica_groups=[list(range(NCORES_AG))], ins=[ag_in[:]], outs=[ag_out[:]],
       sem=s_cc, inc=1, reads=(ag_in_res,), writes=(ag_out_r,))
    ag2_in_r = R("ag2_in")
    ge("gpsimd", "dma_start", out=ag2_in[:], in_=ones_t[0:1, 0:16],
       reads=(ones_r,), writes=(ag2_in_r,), sem=s_w, inc=16)
    ge("gpsimd", "collective_compute", "AllGather", ALU.bypass,
       replica_groups=[list(range(NCORES_AG))], ins=[ag2_in[:]], outs=[ag2_out[:]],
       sem=s_cc, inc=1, reads=(ag2_in_r,), writes=(ag_out_r,))
    for e in range(E):
        src = bass.AP(ag_out, e * TC, [[E * TC, NCORES_AG], [128, 16], [1, 128]])
        ge("gpsimd", "dma_start", out=Lp[:, e, :], in_=src, reads=(ag_out_r,),
           writes=(Lp_r,), sem=s_w, inc=16)
    if debug:
        ge("gpsimd", "dma_start", out=dbg["d_ag2"][:], in_=ag_out[:, :],
           reads=(ag_out_r,), sem=s_w, inc=16)
    ge("vector", "tensor_copy", out=mx_t[:], in_=Lp[:, 0, :], reads=(Lp_r,), writes=(mx_r,), mark=False)
    for e in range(1, E):
        ge("vector", "tensor_tensor", out=mx_t[:], in0=mx_t[:], in1=Lp[:, e, :], op=ALU.max,
           reads=(Lp_r, mx_r), writes=(mx_r,), mark=(e == E - 1))
    for e in range(E):
        ge("vector", "tensor_tensor", out=pp[:, e, :], in0=Lp[:, e, :], in1=mx_t[:],
           op=ALU.subtract, reads=(Lp_r, mx_r), writes=(pp_r,))
        ge("scalar", "activation", out=pp[:, e, :], in_=pp[:, e, :], func=AF.Exp,
           reads=(pp_r,), writes=(pp_r,), mark=(e == E - 1))
    ge("vector", "tensor_copy", out=mx_t[:], in_=pp[:, 0, :], reads=(pp_r,), writes=(mx_r,), mark=False)
    for e in range(1, E):
        ge("vector", "tensor_tensor", out=mx_t[:], in0=mx_t[:], in1=pp[:, e, :], op=ALU.add,
           reads=(pp_r, mx_r), writes=(mx_r,), mark=(e == E - 1))
    ge("scalar", "activation", out=scrB[:], in_=mx_t[:], func=AF.Ln,
       reads=(mx_r,), writes=(scrB_r,), mark=False)
    ge("scalar", "activation", out=mx_t[:], in_=scrB[:], func=AF.Exp, scale=-1.0,
       reads=(scrB_r,), writes=(mx_r,))
    for e in range(E):
        ge("vector", "tensor_tensor", out=pp[:, e, :], in0=pp[:, e, :], in1=mx_t[:],
           op=ALU.mult, reads=(pp_r, mx_r), writes=(pp_r,), mark=(e == E - 1))
    ge("vector", "memset", lo_t[:], 0.0, writes=(lo_r,), mark=False)
    ge("vector", "memset", hi_t[:], 1.0, writes=(hi_r,), mark=False)
    ge("vector", "memset", mid_t[:], 0.5, writes=(mid_r,))
    for it in range(NITER):
        for e in range(E):
            ge("vector", "tensor_scalar", out=scrP[:, e * 128:(e + 1) * 128],
               in0=pp[:, e, :], scalar1=mid_t[:, e:e + 1], scalar2=None, op0=ALU.is_ge,
               op1=ALU.add, accum_out=part_t[:, e:e + 1],
               reads=(pp_r, mid_r), writes=(scrP_r, part_r), mark=(e == E - 1))
        ge("tensor", "matmul", out=pbX[:, 0:E], lhsT=ones_t[:, :], rhs=part_t[:],
           start=True, stop=True, reads=(part_r, ones_r), writes=(prX,))
        ge("vector", "tensor_single_scalar", out=ge_t[:], in_=pbX[:, 0:E],
           scalar=float(CAP), op=ALU.is_ge, reads=(prX,), writes=(geT_r,), mark=False)
        ge("vector", "tensor_tensor", out=sA8[:], in0=mid_t[:], in1=lo_t[:],
           op=ALU.subtract, reads=(mid_r, lo_r), writes=(sA8_r,), mark=False)
        ge("vector", "tensor_tensor", out=sB8[:], in0=hi_t[:], in1=mid_t[:],
           op=ALU.subtract, reads=(hi_r, mid_r), writes=(sB8_r,), mark=False)
        ge("vector", "tensor_tensor", out=sA8[:], in0=ge_t[:], in1=sA8[:],
           op=ALU.mult, reads=(geT_r, sA8_r), writes=(sA8_r,), mark=False)
        ge("vector", "tensor_tensor", out=sB8[:], in0=ge_t[:], in1=sB8[:],
           op=ALU.mult, reads=(geT_r, sB8_r), writes=(sB8_r,), mark=False)
        ge("vector", "tensor_tensor", out=lo_t[:], in0=lo_t[:], in1=sA8[:],
           op=ALU.add, reads=(lo_r, sA8_r), writes=(lo_r,), mark=False)
        ge("vector", "tensor_tensor", out=hi_t[:], in0=mid_t[:], in1=sB8[:],
           op=ALU.add, reads=(mid_r, sB8_r), writes=(hi_r,), mark=False)
        ge("vector", "tensor_tensor", out=sA8[:], in0=lo_t[:], in1=hi_t[:],
           op=ALU.add, reads=(lo_r, hi_r), writes=(sA8_r,), mark=False)
        ge("vector", "tensor_single_scalar", out=mid_t[:], in_=sA8[:], scalar=0.5,
           op=ALU.mult, reads=(sA8_r,), writes=(mid_r,), mark=False)
    for e in range(E):
        ge("vector", "tensor_single_scalar", out=scrA[:], in_=pp[:, e, :],
           scalar=lo_t[:, e:e + 1], op=ALU.is_ge,
           reads=(pp_r, lo_r), writes=(scrA_r,), mark=False)
        ge("vector", "tensor_tensor", out=scrA[:], in0=pp[:, e, :], in1=scrA[:],
           op=ALU.add, reads=(pp_r, scrA_r), writes=(scrA_r,), mark=False)
        if e == 0:
            ge("vector", "tensor_copy", out=best_t[:], in_=scrA[:],
               reads=(scrA_r,), writes=(best_r,), mark=False)
            ge("vector", "memset", bestE_t[:], 0.0, writes=(bestE_r,), mark=False)
        else:
            ge("vector", "tensor_tensor", out=scrB[:], in0=scrA[:], in1=best_t[:],
               op=ALU.is_gt, reads=(scrA_r, best_r), writes=(scrB_r,), mark=False)
            ge("vector", "tensor_tensor", out=scrA[:], in0=scrA[:], in1=best_t[:],
               op=ALU.subtract, reads=(scrA_r, best_r), writes=(scrA_r,), mark=False)
            ge("vector", "tensor_tensor", out=scrA[:], in0=scrB[:], in1=scrA[:],
               op=ALU.mult, reads=(scrB_r, scrA_r), writes=(scrA_r,), mark=False)
            ge("vector", "tensor_tensor", out=best_t[:], in0=best_t[:], in1=scrA[:],
               op=ALU.add, reads=(best_r, scrA_r), writes=(best_r,), mark=False)
            ge("vector", "tensor_scalar", out=scrA[:], in0=bestE_t[:], scalar1=float(e),
               scalar2=-1.0, op0=ALU.subtract, op1=ALU.mult,
               reads=(bestE_r,), writes=(scrA_r,), mark=False)
            ge("vector", "tensor_tensor", out=scrA[:], in0=scrB[:], in1=scrA[:],
               op=ALU.mult, reads=(scrB_r, scrA_r), writes=(scrA_r,), mark=False)
            ge("vector", "tensor_tensor", out=bestE_t[:], in0=bestE_t[:], in1=scrA[:],
               op=ALU.add, reads=(bestE_r, scrA_r), writes=(bestE_r,), mark=False)
    for e in range(E):
        ge("vector", "tensor_single_scalar", out=scrA[:], in_=bestE_t[:],
           scalar=float(e), op=ALU.is_equal, reads=(bestE_r,), writes=(scrA_r,), mark=False)
        ge("vector", "tensor_tensor", out=Lp[:, e, :], in0=scrA[:], in1=pp[:, e, :],
           op=ALU.mult, reads=(scrA_r, pp_r), writes=(Lp_r,), mark=(e == E - 1))
    if debug:
        ge("sync", "dma_start", out=dbg["d_pp"][:], in_=pp[:], reads=(pp_r,), sem=s_in, inc=16)
        ge("sync", "dma_start", out=dbg["d_lo"][:], in_=lo_t[:], reads=(lo_r,), sem=s_in, inc=16)
        ge("sync", "dma_start", out=dbg["d_gate"][:], in_=Lp[:], reads=(Lp_r,), sem=s_in, inc=16)
    for e in range(E):
        ge("tensor", "matmul", out=pbY[:, 0:128], lhsT=Lp[:, e, :], rhs=selm_t[:, e, :],
           start=(e == 0), stop=(e == E - 1), reads=(Lp_r, selm_r), writes=(prY,),
           mark=(e == E - 1))
    ge("vector", "tensor_copy", out=owng_t[:], in_=pbY[:, 0:128],
       reads=(prY,), writes=(owng_r,))

    if debug:
        ge("sync", "dma_start", out=dbg["d_own"][:], in_=owng_t[:], reads=(owng_r,), sem=s_in, inc=16)
        ge("sync", "dma_start", out=dbg["d_yacc"][:], in_=yacc[:], reads=(yacc_r,), sem=s_in, inc=16)

    ge("tensor", "matmul", out=pbX[:, 0:128], lhsT=bestE_t[:], rhs=selm_t[:, 0, :],
       start=True, stop=True, reads=(bestE_r, selm_r), writes=(prX,))
    ge("vector", "tensor_copy", out=ownB[:], in_=pbX[:, 0:16], reads=(prX,), writes=(ownB_r,))
    ge("vector", "tensor_copy", out=gate_tm[:], in_=owng_t[:, 0:16],
       reads=(owng_r,), writes=(gate_tm_r,), mark=False)
    for e in range(1, E):
        ge("vector", "tensor_tensor", out=gate_tm[:], in0=gate_tm[:],
           in1=owng_t[:, e * 16:(e + 1) * 16], op=ALU.add,
           reads=(owng_r, gate_tm_r), writes=(gate_tm_r,), mark=False)
    ge("vector", "memset", zz16[:], 0.0, writes=(zz16_r,), mark=False)
    ge("vector", "memset", slot_f[:], 0.0, writes=(slot_fr,))
    for e in range(E):
        ge("vector", "tensor_single_scalar", out=match_t[:], in_=ownB[:],
           scalar=float(e), op=ALU.is_equal, reads=(ownB_r,), writes=(match_r,))
        ge("vector", "tensor_tensor_scan", out=scn_t[:], data0=match_t[:],
           data1=zz16[:], initial=0.0, op0=ALU.add, op1=ALU.add,
           reads=(match_r, zz16_r), writes=(scn_r,))
        ge("tensor", "matmul", out=pbY[:, 0:1], lhsT=lmask_t[:], rhs=scn_t[:, 15:16],
           start=True, stop=True, reads=(lmask_r, scn_r), writes=(prY,))
        ge("vector", "tensor_copy", out=pref_t[:], in_=pbY[:, 0:1],
           reads=(prY,), writes=(pref_r,))
        ge("vector", "tensor_tensor", out=scn_t[:], in0=scn_t[:], in1=match_t[:],
           op=ALU.subtract, reads=(scn_r, match_r), writes=(scn_r,))
        ge("vector", "tensor_scalar", out=scn_t[:], in0=scn_t[:], scalar1=pref_t[:],
           scalar2=float(e * CAPC), op0=ALU.add, op1=ALU.add,
           reads=(scn_r, pref_r), writes=(scn_r,))
        ge("vector", "tensor_single_scalar", out=scn_t[:], in_=scn_t[:],
           scalar=float(e * CAPC + CAPC - 1), op=ALU.min,
           reads=(scn_r,), writes=(scn_r,))
        ge("vector", "tensor_tensor", out=scn_t[:], in0=scn_t[:], in1=match_t[:],
           op=ALU.mult, reads=(scn_r, match_r), writes=(scn_r,))
        ge("vector", "tensor_tensor", out=slot_f[:], in0=slot_f[:], in1=scn_t[:],
           op=ALU.add, reads=(slot_fr, scn_r), writes=(slot_fr,))
    ge("vector", "tensor_copy", out=slot_i[:], in_=slot_f[:],
       reads=(slot_fr,), writes=(slot_r,))
    xsort_r = R("xsort"); ysort_r = R("ysort")
    for tc in range(16):
        ge("gpsimd", "indirect_dma_start", out=xsort[0:128, :],
           out_offset=IndirectOffsetOnAxis(slot_i[:, tc:tc + 1], 0),
           in_=yacc[:, tc, :], in_offset=None,
           reads=(yacc_r, slot_r), writes=(xsort_r,), sem=s_xe, inc=16)

    scopeR.close()
    stackS.close()
    w1b = EB(nc.sbuf_tensor("w1b", [128, NCHUNK, FF], BF16)); w1b_r = R("w1b")
    w2b = EB(nc.sbuf_tensor("w2b", [128, FF // 128, D], BF16)); w2b_r = R("w2b")
    h16 = EB(nc.sbuf_tensor("h16", [128, FF // 128, 512], BF16)); h16_r = R("h16")

    xet = [EB(nc.sbuf_tensor(f"xet{i}", [128, D], F32)) for i in range(4)]
    xet_r = [R(f"xet{i}") for i in range(4)]
    xe_fm = EB(nc.sbuf_tensor("xe_fm", [128, NCHUNK, CAPC], BF16)); xe_fm_r = R("xe_fm")
    ysb = EB(nc.sbuf_tensor("ysb", [128, CAPC // 128, D], F32)); ysb_r = R("ysb")
    ytok = [EB(nc.sbuf_tensor(f"ytok{i}", [128, D], F32)) for i in range(2)]
    ytok_r = [R(f"ytok{i}") for i in range(2)]
    NT = CAPC // 128
    for e in range(E):
        for tl in range(NT):
            ge("gpsimd", "dma_start", out=xet[tl][:],
               in_=xsort[e * CAPC + tl * 128: e * CAPC + (tl + 1) * 128, :],
               reads=(xsort_r,), writes=(xet_r[tl],), sem=s_xe, inc=16)
        ge("gpsimd", "dma_start", out=w1b[:],
           in_=w1[e, :, :].rearrange("(c p) f -> p c f", p=128),
           reads=(sguard,), writes=(w1b_r,), sem=s_w1, inc=16)
        ge("gpsimd", "dma_start", out=w2b[:],
           in_=w2[e, :, :].rearrange("(c p) d -> p c d", p=128),
           reads=(sguard,), writes=(w2b_r,), sem=s_w2, inc=16)
        for tl in range(NT):
            cur = tl
            j = tl % 3
            for ci in range(NCHUNK):
                ge("tensor", "transpose", out=ps2[j][:, ci * 128:(ci + 1) * 128],
                   in_=xet[cur][:, ci * 128:(ci + 1) * 128], identity=idf_t[:],
                   reads=(xet_r[cur], idf_r), writes=(pr2[j],), mark=(ci == NCHUNK - 1))
            ge("vector", "tensor_copy", out=xe_fm[:, :, tl * 128:(tl + 1) * 128],
               in_=ps2[j][:].rearrange("p (c t) -> p c t", c=NCHUNK),
               reads=(pr2[j],), writes=(xe_fm_r,))
        for hcp in range(FF // 256):
            j = hcp % 3
            for hh2 in range(2):
                hc = 2 * hcp + hh2
                dst = ps2[j][:, hh2 * 512:(hh2 + 1) * 512]
                for ci in range(NCHUNK):
                    ge("tensor", "matmul", out=dst,
                       lhsT=w1b[:, ci, hc * 128:(hc + 1) * 128], rhs=xe_fm[:, ci, :],
                       start=(ci == 0), stop=(ci == NCHUNK - 1),
                       reads=(w1b_r, xe_fm_r), writes=(pr2[j],),
                       mark=(ci == NCHUNK - 1 and hh2 == 1))
            ge("scalar", "activation", out=h16[:, 2 * hcp:2 * hcp + 2, :],
               in_=ps2[j][:].rearrange("p (a b) -> p a b", a=2), func=AF.Gelu,
               reads=(pr2[j],), writes=(h16_r,))
        for tc4 in range(NT):
            j = tc4 % 3
            for hc in range(FF // 128):
                ge("tensor", "matmul", out=ps2[j][:, 0:512],
                   lhsT=h16[:, hc, tc4 * 128:(tc4 + 1) * 128],
                   rhs=w2b[:, hc, 0:512],
                   start=(hc == 0), stop=(hc == FF // 128 - 1),
                   reads=(h16_r, w2b_r), writes=(pr2[j],), mark=False)
                ge("tensor", "matmul", out=ps2[j][:, 512:1024],
                   lhsT=h16[:, hc, tc4 * 128:(tc4 + 1) * 128],
                   rhs=w2b[:, hc, 512:1024],
                   start=(hc == 0), stop=(hc == FF // 128 - 1),
                   reads=(h16_r, w2b_r), writes=(pr2[j],),
                   mark=(hc == FF // 128 - 1))
            ge("vector", "tensor_copy", out=ysb[:, tc4, :], in_=ps2[j][:],
               reads=(pr2[j],), writes=(ysb_r,))
        ge("sync", "dma_start",
           out=ysort[e * CAPC:(e + 1) * CAPC, :].rearrange("(tl p) d -> p tl d", p=128),
           in_=ysb[:], reads=(ysb_r,), writes=(ysort_r,), sem=s_ys, inc=16)

    for tc in range(16):
        cur = tc % 2
        statc, statc_r = statc2[cur], statc2_r[cur]
        ge("gpsimd", "indirect_dma_start", out=ytok[cur][:], out_offset=None,
           in_=ysort[0:128, :], in_offset=IndirectOffsetOnAxis(slot_i[:, tc:tc + 1], 0),
           reads=(ysort_r, slot_r), writes=(ytok_r[cur],), sem=s_w, inc=16)
        ge("vector", "scalar_tensor_tensor", out=yacc[:, tc, :], in0=ytok[cur][:],
           scalar=gate_tm[:, tc:tc + 1], in1=yacc[:, tc, :], op0=ALU.mult, op1=ALU.add,
           reads=(ytok_r[cur], gate_tm_r, yacc_r), writes=(yacc_r,))
        ge("scalar", "activation", out=sq2[:], in_=yacc[:, tc, :], func=AF.Identity,
           accum_out=statc[:, 0:1], reads=(yacc_r,), writes=(sq2_r, statc_r), mark=False)
        ge("scalar", "activation", out=sq2[:], in_=yacc[:, tc, :], func=AF.Square,
           accum_out=statc[:, 1:2], reads=(yacc_r,), writes=(sq2_r, statc_r))
        ge("vector", "tensor_single_scalar", out=statc[:, 2:3], in_=statc[:, 0:1],
           scalar=1.0 / D, op=ALU.mult, reads=(statc_r,), writes=(statc_r,), mark=False)
        ge("vector", "tensor_single_scalar", out=statc[:, 3:4], in_=statc[:, 1:2],
           scalar=1.0 / D, op=ALU.mult, reads=(statc_r,), writes=(statc_r,), mark=False)
        ge("vector", "tensor_tensor", out=statc[:, 4:5], in0=statc[:, 2:3],
           in1=statc[:, 2:3], op=ALU.mult, reads=(statc_r,), writes=(statc_r,), mark=False)
        ge("vector", "tensor_tensor", out=statc[:, 4:5], in0=statc[:, 3:4],
           in1=statc[:, 4:5], op=ALU.subtract, reads=(statc_r,), writes=(statc_r,))
        ge("scalar", "activation", out=statc[:, 4:5], in_=statc[:, 4:5], func=AF.Ln,
           bias=eps_t[:], reads=(statc_r, eps_r), writes=(statc_r,), mark=False)
        ge("scalar", "activation", out=statc[:, 5:6], in_=statc[:, 4:5], func=AF.Exp,
           scale=-0.5, reads=(statc_r,), writes=(statc_r,))
        ge("vector", "tensor_scalar", out=yacc[:, tc, :], in0=yacc[:, tc, :],
           scalar1=statc[:, 2:3], scalar2=statc[:, 5:6], op0=ALU.subtract, op1=ALU.mult,
           reads=(yacc_r, statc_r), writes=(yacc_r,))
        ge("sync", "dma_start", out=yout[tc * 128:(tc + 1) * 128, :], in_=yacc[:, tc, :],
           reads=(yacc_r,), sem=s_in, inc=16)

    P.barrier_from("sync", s_in, s_in.v)
    scopeB.close()

    with nc.Block() as block:
        P.play(block)
    ctx.close()
    return nc

NCORES_AG = 8


def kernel(L0_attn_hidden_states, L0_attn_Wq, L0_attn_bq, L0_attn_Wk,
           L0_attn_Wv, L0_attn_bv, L0_attn_Wo, L0_attn_bo,
           L0_ffn_Wg, L0_ffn_W1, L0_ffn_W2):
    x = np.ascontiguousarray(np.asarray(L0_attn_hidden_states, dtype=np.float32))
    wqT = np.ascontiguousarray(np.asarray(L0_attn_Wq, np.float32).T)
    wkT = np.ascontiguousarray(np.asarray(L0_attn_Wk, np.float32).T)
    wvT = np.ascontiguousarray(np.asarray(L0_attn_Wv, np.float32).T)
    woT = np.ascontiguousarray(np.asarray(L0_attn_Wo, np.float32).T)
    bq2 = np.ascontiguousarray(np.asarray(L0_attn_bq, np.float32).reshape(NCHUNK, 128).T)
    bo2 = np.ascontiguousarray(np.asarray(L0_attn_bo, np.float32).reshape(NCHUNK, 128).T)
    bvr = np.ascontiguousarray(np.asarray(L0_attn_bv, np.float32).reshape(1, HD))
    wg = np.ascontiguousarray(np.asarray(L0_ffn_Wg, np.float32))
    w1 = np.ascontiguousarray(np.asarray(L0_ffn_W1, np.float32))
    w2 = np.ascontiguousarray(np.asarray(L0_ffn_W2, np.float32))
    identf = np.eye(128, dtype=np.float32)
    onesf = np.ones((128, 128), dtype=np.float32)
    lmaskf = np.ascontiguousarray(np.triu(np.ones((128, 128), dtype=np.float32), 1))

    nc = build_nc()

    in_maps = []
    for c in range(8):
        b, hh = c // 2, c % 2
        xbat = x[b]
        if hh == 0:
            xb_c = xbat
        else:
            xb_c = np.ascontiguousarray(np.concatenate([xbat[2048:], xbat[:2048]], axis=0))
        selmat = np.zeros((E, 128, 128), dtype=np.float32)
        for e in range(E):
            for j in range(16):
                selmat[e, c * 16 + j, e * 16 + j] = 1.0
        in_maps.append({
            "xb": xb_c, "wqT": wqT, "wkT": wkT, "wvT": wvT, "woT": woT,
            "bq2": bq2, "bo2": bo2, "bvr": bvr, "wg": wg, "w1": w1, "w2": w2,
            "identf": identf, "onesf": onesf, "selmat": selmat, "lmask": lmaskf,
        })

    res = run_bass_kernel_spmd(nc, in_maps, core_ids=list(range(8)))
    out = np.empty((B, T, D), dtype=np.float32)
    for c in range(8):
        b, hh = c // 2, c % 2
        out[b, hh * 2048:(hh + 1) * 2048] = res.results[c]["yout"]
    return out


# version 32
# speedup vs baseline: 1.0217x; 1.0217x over previous
import numpy as np
from contextlib import ExitStack

import concourse.bass as bass
import concourse.mybir as mybir
from concourse.bass import IndirectOffsetOnAxis
from concourse.bass_utils import run_bass_kernel_spmd

F32 = mybir.dt.float32
BF16 = mybir.dt.bfloat16
I32 = mybir.dt.int32
AF = mybir.ActivationFunctionType
ALU = mybir.AluOpType
AX = mybir.AxisListType

B, T, D = 4, 4096, 1024
NH, HD = 16, 64
TC = 2048
NCHUNK = D // 128
NSLAB = T // 128
NG = 4
E, FF = 8, 2048
CAP = (B * T) // E
CAPC = 512
NITER = 18

ENGINES = ("sync", "gpsimd", "tensor", "vector", "scalar")


class Sem:
    def __init__(self, handle, name, dma=False):
        self.h = handle
        self.name = name
        self.v = 0
        self.dma = dma


class Res:
    __slots__ = ("name", "prod", "cons")

    def __init__(self, name):
        self.name = name
        self.prod = None
        self.cons = []


class Planner:
    def __init__(self, nc, sems):
        self.nc = nc
        self.streams = {e: [] for e in ENGINES}
        self.sems = sems
        self.floor = {e: {} for e in ENGINES}
        self.pending = {e: [] for e in ENGINES}
        self.last_written = {e: [set(), set(), set()] for e in ENGINES}
        self.spacer_aps = {}

    def _wait(self, eng, sem, val):
        if sem.dma:
            val = sem.v
        if val <= 0:
            return
        fl = self.floor[eng]
        if fl.get(sem.name, 0) >= val:
            return
        fl[sem.name] = val
        h = sem.h
        self.streams[eng].append(("wait", h, val))

    def emit(self, eng, meth, *args, reads=(), writes=(), sem=None, inc=0,
             mark=True, tag=None, **kw):
        if eng in ("vector", "scalar"):
            hist = self.last_written[eng]
            lw = hist[0] | hist[1] | hist[2]
            if any((r is w) for r in reads for w in lw):
                self._emit_spacer(eng)
        waits = []
        for r in reads:
            if r.prod is not None:
                waits.append(r.prod)
        for r in writes:
            if r.prod is not None:
                waits.append(r.prod)
            waits.extend(r.cons)
        bysem = {}
        for (s, v) in waits:
            if bysem.get(s, 0) < v:
                bysem[s] = v
        for s2, v2 in bysem.items():
            self._wait(eng, s2, v2)

        if sem is None and mark:
            sem = self.sems[eng]
            inc = 1
        if sem is not None:
            sem.v += inc
            markrec = (sem, sem.v)
            if eng in ("tensor", "vector", "scalar") and sem is self.sems[eng]:
                self.floor[eng][sem.name] = sem.v
        else:
            markrec = None

        self.streams[eng].append(("op", meth, args, kw, sem.h if sem else None, inc, tag))

        recs = [(r, "r") for r in reads] + [(r, "w") for r in writes]
        if markrec is None:
            self.pending[eng].extend(recs)
        else:
            for r, kind in self.pending[eng]:
                self._apply(r, kind, markrec)
            self.pending[eng] = []
            for r, kind in recs:
                self._apply(r, kind, markrec)

        if eng in ("vector", "scalar"):
            hist = self.last_written[eng]
            self.last_written[eng] = [set(writes), hist[0], hist[1]]

    def _emit_spacer(self, eng):
        (o1, i1), (o2, i2) = self.spacer_aps[eng]
        meth = "tensor_copy" if eng == "vector" else "copy"
        self.streams[eng].append(("op", meth, (), {"out": o1, "in_": i1}, None, 0, "spacer"))
        self.streams[eng].append(("op", meth, (), {"out": o2, "in_": i2}, None, 0, "spacer"))
        self.last_written[eng] = [set(), set(), set()]

    @staticmethod
    def _apply(r, kind, markrec):
        if kind == "r":
            r.cons.append(markrec)
        else:
            r.prod = markrec
            r.cons = []

    def barrier_from(self, eng, other_sem, val):
        self._wait(eng, other_sem, val)

    def scope_barrier(self, sems):
        targets = [(s, s.v) for s in sems if s.v > 0]
        for eng in ENGINES:
            for s, v in targets:
                self._wait(eng, s, v)

    def play(self, block):
        def mk(eng):
            def run(handle):
                for item in self.streams[eng]:
                    if item[0] == "wait":
                        handle.wait_ge(item[1], item[2])
                    else:
                        _, meth, args, kw, semh, inc = item[:6]
                        ins = getattr(handle, meth)(*args, **kw)
                        if semh is not None:
                            ins.then_inc(semh, inc)
            return run
        block.sync(mk("sync"))
        block.gpsimd(mk("gpsimd"))
        block.tensor(mk("tensor"))
        block.vector(mk("vector"))
        block.scalar(mk("scalar"))

def build_nc(debug=False):
    nc = bass.Bass("TRN2", target_bir_lowering=False)

    xb = nc.dram_tensor("xb", [T, D], F32, kind="ExternalInput")
    wqT = nc.dram_tensor("wqT", [D, D], F32, kind="ExternalInput")
    wkT = nc.dram_tensor("wkT", [D, HD], F32, kind="ExternalInput")
    wvT = nc.dram_tensor("wvT", [D, HD], F32, kind="ExternalInput")
    woT = nc.dram_tensor("woT", [D, D], F32, kind="ExternalInput")
    bq2 = nc.dram_tensor("bq2", [128, NCHUNK], F32, kind="ExternalInput")
    bo2 = nc.dram_tensor("bo2", [128, NCHUNK], F32, kind="ExternalInput")
    bvr = nc.dram_tensor("bvr", [1, HD], F32, kind="ExternalInput")
    wg = nc.dram_tensor("wg", [D, E], F32, kind="ExternalInput")
    w1 = nc.dram_tensor("w1", [E, D, FF], F32, kind="ExternalInput")
    w2 = nc.dram_tensor("w2", [E, FF, D], F32, kind="ExternalInput")
    identf = nc.dram_tensor("identf", [128, 128], F32, kind="ExternalInput")
    onesf = nc.dram_tensor("onesf", [128, 128], F32, kind="ExternalInput")
    selmat = nc.dram_tensor("selmat", [E, 128, 128], F32, kind="ExternalInput")
    yout = nc.dram_tensor("yout", [TC, D], F32, kind="ExternalOutput")

    dbg = {}
    if debug:
        for name, shp in [("d_S0", [128, NCHUNK, TC]), ("d_S1", [128, NCHUNK, TC]),
                          ("d_logit", [E, TC]), ("d_pp", [128, E, 128]),
                          ("d_lo", [128, E]), ("d_own", [128, 128]),
                          ("d_gate", [128, E, 128]), ("d_yacc", [128, 16, D]),
                          ("d_ag2", [NCORES_AG * E, TC])]:
            dbg[name] = nc.dram_tensor(name, shp, F32, kind="ExternalOutput")

    xsort = nc.dram_tensor("xsort", [E * CAPC, D], F32)
    ysort = nc.dram_tensor("ysort", [E * CAPC, D], F32)
    lmask = nc.dram_tensor("lmask", [128, 128], F32, kind="ExternalInput")
    ag_in = nc.dram_tensor("ag_in", [E, TC], F32)
    ag_out = nc.dram_tensor("ag_out", [NCORES_AG * E, TC], F32, addr_space="Shared")
    ag2_in = nc.dram_tensor("ag2_in", [1, 16], F32)
    ag2_out = nc.dram_tensor("ag2_out", [NCORES_AG, 16], F32, addr_space="Shared")

    ctx = ExitStack()
    EE = ctx.enter_context

    s_in = Sem(EE(nc.semaphore("s_in")), "s_in", dma=True)
    s_w = Sem(EE(nc.semaphore("s_w")), "s_w", dma=True)
    s_w1 = Sem(EE(nc.semaphore("s_w1")), "s_w1", dma=True)
    s_w2 = Sem(EE(nc.semaphore("s_w2")), "s_w2", dma=True)
    s_x = [Sem(EE(nc.semaphore(f"s_x{i}")), f"s_x{i}", dma=True) for i in range(3)]
    s_xe = Sem(EE(nc.semaphore("s_xe")), "s_xe", dma=True)
    s_ys = Sem(EE(nc.semaphore("s_ys")), "s_ys", dma=True)
    s_pe = Sem(EE(nc.semaphore("s_pe")), "s_pe")
    s_act = Sem(EE(nc.semaphore("s_act")), "s_act")
    s_dve = Sem(EE(nc.semaphore("s_dve")), "s_dve")
    s_cc = Sem(EE(nc.semaphore("s_cc")), "s_cc", dma=True)

    P = Planner(nc, {"tensor": s_pe, "scalar": s_act, "vector": s_dve,
                     "sync": s_in, "gpsimd": s_w})

    def R(name):
        return Res(name)

    spc_a = EE(nc.sbuf_tensor("spc_a", [1, 256], F32))
    spc_b = EE(nc.sbuf_tensor("spc_b", [1, 256], F32))

    ps2 = [EE(nc.psum_tensor(f"ps2_{i}", [128, 1024], F32)) for i in range(3)]
    pr2 = [R(f"ps2_{i}") for i in range(3)]
    pbX = EE(nc.psum_tensor("pbX", [128, 512], F32)); prX = R("pbX")
    pbY = EE(nc.psum_tensor("pbY", [128, 512], F32)); prY = R("pbY")

    bq2s_t = EE(nc.sbuf_tensor("bq2s", [128, NCHUNK], F32)); bq2s_r = R("bq2s")
    bo2_t = EE(nc.sbuf_tensor("bo2t", [128, NCHUNK], F32)); bo2_r = R("bo2t")
    bvb_t = EE(nc.sbuf_tensor("bvb", [128, HD], F32)); bvb_r = R("bvb")
    bvr_t = EE(nc.sbuf_tensor("bvrt", [1, HD], F32)); bvr_r = R("bvrt")
    idf_t = EE(nc.sbuf_tensor("idf", [128, 128], F32)); idf_r = R("idf")
    ones_t = EE(nc.sbuf_tensor("ones", [128, 128], F32)); ones_r = R("ones")
    P.spacer_aps["vector"] = ((spc_a[0:1, 0:64], spc_a[0:1, 128:192]),
                              (spc_a[0:1, 64:128], spc_a[0:1, 192:256]))
    P.spacer_aps["scalar"] = ((spc_b[0:1, 0:64], spc_b[0:1, 128:192]),
                              (spc_b[0:1, 64:128], spc_b[0:1, 192:256]))
    wgf_t = EE(nc.sbuf_tensor("wgf", [128, NCHUNK, E], F32)); wgf_r = R("wgf")
    selm_t = EE(nc.sbuf_tensor("selm", [128, E, 128], F32)); selm_r = R("selm")
    owng_t = EE(nc.sbuf_tensor("owng", [128, 128], F32)); owng_r = R("owng")
    eps_t = EE(nc.sbuf_tensor("epsT", [128, 1], F32)); eps_r = R("epsT")
    wgs_t = EE(nc.sbuf_tensor("wgs", [E, 1], F32)); wgs_r = R("wgs")
    lmask_t = EE(nc.sbuf_tensor("lmask_t", [128, 128], F32)); lmask_r = R("lmask_t")
    ones16_t = EE(nc.sbuf_tensor("ones16", [128, 64], BF16)); ones16_r = R("ones16")
    rowsb_t = EE(nc.sbuf_tensor("rowsb", [1, 1024], BF16)); rowsb_r = R("rowsb")
    gate_tm = EE(nc.sbuf_tensor("gate_tm", [128, 16], F32)); gate_tm_r = R("gate_tm")
    slot_i = EE(nc.sbuf_tensor("slot_i", [128, 16], I32)); slot_r = R("slot_i")

    stackS = ExitStack()
    S_t = stackS.enter_context(nc.sbuf_tensor("S", [128, NCHUNK, TC], F32, side="right"))
    S_r = R("S")

    ge = P.emit
    ge("sync", "dma_start", out=bq2s_t[:], in_=bq2[:], reads=(), writes=(bq2s_r,), sem=s_in, inc=16)
    ge("sync", "dma_start", out=bo2_t[:], in_=bo2[:], writes=(bo2_r,), sem=s_in, inc=16)
    ge("sync", "dma_start", out=bvr_t[:], in_=bvr[:], writes=(bvr_r,), sem=s_in, inc=16)
    ge("sync", "dma_start", out=idf_t[:], in_=identf[:], writes=(idf_r,), sem=s_in, inc=16)
    ge("sync", "dma_start", out=ones_t[:], in_=onesf[:], writes=(ones_r,), sem=s_in, inc=16)
    ge("sync", "dma_start", out=wgf_t[:],
       in_=wg[:, :].rearrange("(c p) e -> p c e", p=128), writes=(wgf_r,), sem=s_in, inc=16)
    ge("sync", "dma_start", out=selm_t[:], in_=selmat[:, :, :].rearrange("e p f -> p e f"),
       writes=(selm_r,), sem=s_in, inc=16)
    ge("sync", "dma_start", out=lmask_t[:], in_=lmask[:], writes=(lmask_r,), sem=s_in, inc=16)

    ge("vector", "memset", spc_a[:], 0.0)
    ge("scalar", "memzero", spc_b[:])
    ge("vector", "memset", eps_t[:], 1e-5, writes=(eps_r,))
    ge("vector", "tensor_copy", out=ones16_t[:], in_=ones_t[:, 0:64],
       reads=(ones_r,), writes=(ones16_r,))
    ge("vector", "tensor_single_scalar", out=bq2s_t[:], in_=bq2s_t[:], scalar=0.125,
       op=ALU.mult, reads=(bq2s_r,), writes=(bq2s_r,))
    for ci in range(NCHUNK):
        ge("tensor", "matmul", out=pbY[0:E, 0:1], lhsT=wgf_t[:, ci, :], rhs=ones_t[:, 0:1],
           start=(ci == 0), stop=(ci == NCHUNK - 1), reads=(ones_r, wgf_r),
           writes=(prY,), mark=(ci == NCHUNK - 1))
    ge("vector", "tensor_copy", out=wgs_t[:], in_=pbY[0:E, 0:1], reads=(prY,), writes=(wgs_r,))
    ge("tensor", "matmul", out=pbX[:, 0:HD], lhsT=ones_t[0:1, :], rhs=bvr_t[:],
       start=True, stop=True, reads=(ones_r, bvr_r), writes=(prX,))
    ge("vector", "tensor_copy", out=bvb_t[:], in_=pbX[:, 0:HD],
       reads=(prX,), writes=(bvb_r,))

    scopeA = ExitStack()
    EA = scopeA.enter_context
    wq16 = EA(nc.sbuf_tensor("wq16", [128, NCHUNK, D], BF16)); wq16_r = R("wq16")
    wo16 = EA(nc.sbuf_tensor("wo16", [128, NCHUNK, D], BF16)); wo16_r = R("wo16")
    wk16 = EA(nc.sbuf_tensor("wk16", [128, NCHUNK, HD], BF16)); wk16_r = R("wk16")
    wv16 = EA(nc.sbuf_tensor("wv16", [128, NCHUNK, HD], BF16)); wv16_r = R("wv16")
    xslab = [EA(nc.sbuf_tensor(f"xslab{i}", [128, D], F32)) for i in range(3)]
    xslab_r = [R(f"xslab{i}") for i in range(3)]
    xfm16 = [EA(nc.sbuf_tensor(f"xfm16_{i}", [128, NCHUNK, 128], BF16)) for i in range(2)]
    xfm16_r = [R(f"xfm16_{i}") for i in range(2)]
    qg16 = EA(nc.sbuf_tensor("qg16", [128, NH // 2, 512], BF16)); qg16_r = R("qg16")
    Pb2 = [EA(nc.sbuf_tensor(f"Pb2_{i}", [128, 1024], BF16)) for i in range(3)]
    Pb2_r = [R(f"Pb2_{i}") for i in range(3)]
    ctx16 = EA(nc.sbuf_tensor("ctx16", [128, NCHUNK, 512], BF16)); ctx16_r = R("ctx16")
    x0g16, x0g16_r = ctx16, ctx16_r
    sq_sb = EA(nc.sbuf_tensor("sq_sb", [128, 512], F32)); sq_sb_r = R("sq_sb")
    rows_t = EA(nc.sbuf_tensor("rows", [1, 1024], F32)); rows_r = R("rows")
    rb = [EA(nc.sbuf_tensor(f"rb{i}", [64, 512], F32)) for i in range(2)]
    rb_r = [R(f"rb{i}") for i in range(2)]
    Kd = EA(nc.sbuf_tensor("Kd", [128, T], BF16)); Kd_r = R("Kd")
    Vt = EA(nc.sbuf_tensor("Vt", [128, NSLAB * (HD + 1)], BF16)); Vt_r = R("Vt")
    logit_sb = EA(nc.sbuf_tensor("logit_sb", [E, TC], F32)); logit_r = R("logit_sb")

    ge("gpsimd", "dma_start", out=wq16[:],
       in_=wqT[:, :].rearrange("(c p) o -> p c o", p=128), writes=(wq16_r,), sem=s_w, inc=16)
    ge("gpsimd", "dma_start", out=wk16[:],
       in_=wkT[:, :].rearrange("(c p) o -> p c o", p=128), writes=(wk16_r,), sem=s_w, inc=16)
    ge("gpsimd", "dma_start", out=wv16[:],
       in_=wvT[:, :].rearrange("(c p) o -> p c o", p=128), writes=(wv16_r,), sem=s_w, inc=16)
    ge("gpsimd", "dma_start", out=wo16[:],
       in_=woT[:, :].rearrange("(c p) o -> p c o", p=128), writes=(wo16_r,), sem=s_w, inc=16)

    ge("vector", "memset", Vt[:], 1.0, writes=(Vt_r,))

    for si in range(NSLAB):
        cur = si % 3
        fcur = si % 2
        jt = si % 3
        ge("sync", "dma_start", out=xslab[cur][:], in_=xb[si * 128:(si + 1) * 128, :],
           reads=(), writes=(xslab_r[cur],), sem=s_x[cur], inc=16)
        for ci in range(NCHUNK):
            ge("tensor", "transpose", out=ps2[jt][:, ci * 128:(ci + 1) * 128],
               in_=xslab[cur][:, ci * 128:(ci + 1) * 128], identity=idf_t[:],
               reads=(xslab_r[cur], idf_r), writes=(pr2[jt],), mark=(ci == NCHUNK - 1))
        ge("vector", "tensor_copy", out=xfm16[fcur][:, :, :],
           in_=ps2[jt][:].rearrange("p (c t) -> p c t", c=NCHUNK),
           reads=(pr2[jt],), writes=(xfm16_r[fcur],), mark=(si >= 16))
        if si < 16:
            ge("vector", "tensor_copy", out=S_t[:, :, si * 128:(si + 1) * 128],
               in_=ps2[jt][:].rearrange("p (c t) -> p c t", c=NCHUNK),
               reads=(pr2[jt],), writes=(S_r,))
        for ci in range(NCHUNK):
            ge("tensor", "matmul", out=pbX[0:64, 0:128], lhsT=wk16[:, ci, :],
               rhs=xfm16[fcur][:, ci, :], start=(ci == 0), stop=(ci == NCHUNK - 1),
               reads=(wk16_r, xfm16_r[fcur]), writes=(prX,), mark=(ci == NCHUNK - 1))
        ge("vector", "tensor_copy", out=Kd[0:64, si * 128:(si + 1) * 128],
           in_=pbX[0:64, 0:128], reads=(prX,), writes=(Kd_r,), mark=False)
        ge("vector", "tensor_copy", out=Kd[64:128, si * 128:(si + 1) * 128],
           in_=pbX[0:64, 0:128], reads=(prX,), writes=(Kd_r,))
        for ci in range(NCHUNK):
            ge("tensor", "matmul", out=pbY[:, 0:HD], lhsT=xfm16[fcur][:, ci, :],
               rhs=wv16[:, ci, :], start=(ci == 0), stop=(ci == NCHUNK - 1),
               reads=(wv16_r, xfm16_r[fcur]), writes=(prY,), mark=(ci == NCHUNK - 1))
        ge("vector", "tensor_tensor",
           out=Vt[:, si * (HD + 1): si * (HD + 1) + HD], in0=pbY[:, 0:HD],
           in1=bvb_t[:], op=ALU.add, reads=(prY, bvb_r), writes=(Vt_r,))

    if debug:
        ge("sync", "dma_start", out=dbg["d_S0"][:], in_=S_t[:], reads=(S_r,), sem=s_in, inc=16)

    GS = 512

    def emit_x0q(g):
        gsl = slice(g * GS, (g + 1) * GS)
        ge("vector", "tensor_copy", out=x0g16[:, :, :], in_=S_t[:, :, gsl],
           reads=(S_r,), writes=(x0g16_r,))
        for oc in range(NCHUNK):
            j, h = (oc % 6) // 2, (oc % 6) % 2
            dst = ps2[j][:, h * 512:(h + 1) * 512]
            for ci in range(NCHUNK):
                ge("tensor", "matmul", out=dst, lhsT=wq16[:, ci, oc * 128:(oc + 1) * 128],
                   rhs=x0g16[:, ci, :], start=(ci == 0), stop=(ci == NCHUNK - 1),
                   reads=(wq16_r, x0g16_r), writes=(pr2[j],), mark=(ci == NCHUNK - 1))
            ge("scalar", "activation", out=qg16[:, oc, :], in_=dst,
               func=AF.Identity, bias=bq2s_t[:, oc:oc + 1], scale=0.125,
               reads=(pr2[j], bq2s_r), writes=(qg16_r,))

    emit_x0q(0)
    for g in range(NG):
        gsl = slice(g * GS, (g + 1) * GS)

        ctxtgt = [(pbX, prX, pbY, prY),
                  (ps2[2][0:128, 0:512], pr2[2], ps2[2][0:128, 512:1024], pr2[2])]

        def tail_a(hp_):
            cA, rA, cB, rB = ctxtgt[hp_ % 2]
            for par, (cb, cr) in ((0, (cA, rA)), (1, (cB, rB))):
                rsl = slice(0, 512) if par == 0 else slice(512, 1024)
                ge("scalar", "activation", out=rows_t[0:1, rsl], in_=cb[HD:HD + 1, :],
                   func=AF.Ln, reads=(cr,), writes=(rows_r,), mark=False)
                ge("scalar", "activation", out=rowsb_t[0:1, rsl], in_=rows_t[0:1, rsl],
                   func=AF.Exp, scale=-1.0, reads=(rows_r,), writes=(rowsb_r,))
                ge("vector", "tensor_copy", out=rb[par][:], in_=cb[0:HD, :],
                   reads=(cr,), writes=(rb_r[par],))

        def tail_b(hp_):
            cA, rA, cB, rB = ctxtgt[hp_ % 2]
            for par, (cb, cr) in ((0, (cA, rA)), (1, (cB, rB))):
                rsl = slice(0, 512) if par == 0 else slice(512, 1024)
                ge("tensor", "matmul", out=cb[0:HD, :], lhsT=ones16_t[0:1, :],
                   rhs=rowsb_t[0:1, rsl], start=True, stop=True,
                   reads=(rowsb_r, ones16_r, rb_r[par]), writes=(cr,))

        def tail_c(hp_):
            cA, rA, cB, rB = ctxtgt[hp_ % 2]
            for par, (cb, cr) in ((0, (cA, rA)), (1, (cB, rB))):
                ge("vector", "tensor_tensor", out=ctx16[par * 64:(par + 1) * 64, hp_, :],
                   in0=rb[par][:], in1=cb[0:HD, :], op=ALU.mult,
                   reads=(cr, rb_r[par]), writes=(ctx16_r,))

        for hp in range(NH // 2):
            cA, rA, cB, rB = ctxtgt[hp % 2]
            for scs in range(NSLAB + 2):
                if scs < NSLAB:
                    sc = scs
                    j = sc % 2
                    jp = sc % 3
                    ksl = slice(sc * 128, (sc + 1) * 128)
                    ge("tensor", "matmul", out=ps2[j][:, 0:512], lhsT=Kd[0:64, ksl],
                       rhs=qg16[0:64, hp, :], start=True, stop=True, tile_position=(0, 0),
                       reads=(Kd_r, qg16_r), writes=(pr2[j],), mark=False)
                    ge("tensor", "matmul", out=ps2[j][:, 512:1024], lhsT=Kd[64:128, ksl],
                       rhs=qg16[64:128, hp, :], start=True, stop=True, tile_position=(64, 0),
                       reads=(Kd_r, qg16_r), writes=(pr2[j],))
                    ge("scalar", "activation", out=Pb2[jp][:], in_=ps2[j][:], func=AF.Exp,
                       reads=(pr2[j],), writes=(Pb2_r[jp],))
                if hp > 0 and scs == 0:
                    tail_a(hp - 1)
                if hp > 0 and scs == 1:
                    tail_b(hp - 1)
                if hp > 0 and scs == 2:
                    tail_c(hp - 1)
                if scs >= 2:
                    sc = scs - 2
                    jp = sc % 3
                    vsl = slice(sc * (HD + 1), (sc + 1) * (HD + 1))
                    ge("tensor", "matmul", out=cA[0:HD + 1, :], lhsT=Vt[:, vsl],
                       rhs=Pb2[jp][:, 0:512], start=(sc == 0), stop=(sc == NSLAB - 1),
                       skip_group_check=True, reads=(Vt_r, Pb2_r[jp]), writes=(rA,), mark=False)
                    ge("tensor", "matmul", out=cB[0:HD + 1, :], lhsT=Vt[:, vsl],
                       rhs=Pb2[jp][:, 512:1024], start=(sc == 0), stop=(sc == NSLAB - 1),
                       skip_group_check=True, reads=(Vt_r, Pb2_r[jp]), writes=(rB,))
        tail_a(NH // 2 - 1)
        tail_b(NH // 2 - 1)
        tail_c(NH // 2 - 1)
        def stats_oc(oc):
            ge("tensor", "matmul", out=pbX[0:1, :], lhsT=ones_t[:, 0:1],
               rhs=S_t[:, oc, gsl], start=(oc == 0), stop=(oc == NCHUNK - 1),
               skip_group_check=True, reads=(S_r, ones_r), writes=(prX,),
               mark=(oc == NCHUNK - 1))
            ge("scalar", "activation", out=sq_sb[:], in_=S_t[:, oc, gsl], func=AF.Square,
               reads=(S_r,), writes=(sq_sb_r,))
            ge("tensor", "matmul", out=pbY[0:1, :], lhsT=ones_t[:, 0:1],
               rhs=sq_sb[:], start=(oc == 0), stop=(oc == NCHUNK - 1),
               skip_group_check=True, reads=(sq_sb_r, ones_r), writes=(prY,))

        for oc in range(NCHUNK + 1):
            if oc < NCHUNK:
                j, h = (oc % 6) // 2, (oc % 6) % 2
                dst = ps2[j][:, h * 512:(h + 1) * 512]
                for ci in range(NCHUNK):
                    ge("tensor", "matmul", out=dst, lhsT=wo16[:, ci, oc * 128:(oc + 1) * 128],
                       rhs=ctx16[:, ci, :], start=(ci == 0), stop=(ci == NCHUNK - 1),
                       reads=(wo16_r, ctx16_r), writes=(pr2[j],), mark=(ci == NCHUNK - 1))
                ge("vector", "scalar_tensor_tensor", out=S_t[:, oc, gsl], in0=dst,
                   scalar=bo2_t[:, oc:oc + 1], in1=S_t[:, oc, gsl], op0=ALU.add, op1=ALU.add,
                   reads=(pr2[j], bo2_r, S_r), writes=(S_r,))
            if oc >= 1:
                stats_oc(oc - 1)
        mrow = rows_t[0:1, 0:512]
        erow = rows_t[0:1, 512:1024]
        vrow = sq_sb[0:1, 0:512]
        rrow = rows_t[0:1, 512:1024]
        ge("vector", "tensor_single_scalar", out=mrow, in_=pbX[0:1, :], scalar=1.0 / D,
           op=ALU.mult, reads=(prX,), writes=(rows_r,), mark=False)
        ge("vector", "tensor_single_scalar", out=erow, in_=pbY[0:1, :], scalar=1.0 / D,
           op=ALU.mult, reads=(prY,), writes=(rows_r,), mark=False)
        ge("vector", "tensor_tensor", out=vrow, in0=mrow, in1=mrow, op=ALU.mult,
           reads=(rows_r,), writes=(sq_sb_r,), mark=False)
        ge("vector", "tensor_tensor", out=vrow, in0=erow, in1=vrow, op=ALU.subtract,
           reads=(rows_r, sq_sb_r), writes=(sq_sb_r,))
        for ci in range(NCHUNK):
            ge("tensor", "matmul", out=pbX[0:E, :], lhsT=wgf_t[:, ci, :],
               rhs=S_t[:, ci, gsl], start=(ci == 0), stop=(ci == NCHUNK - 1),
               reads=(wgf_r, S_r), writes=(prX,), mark=(ci == NCHUNK - 1))
        if g + 1 < NG:
            emit_x0q(g + 1)
        ge("scalar", "activation", out=vrow, in_=vrow, func=AF.Ln, bias=eps_t[0:1, :],
           reads=(sq_sb_r, eps_r), writes=(sq_sb_r,), mark=False)
        ge("scalar", "activation", out=rrow, in_=vrow, func=AF.Exp, scale=-0.5,
           reads=(sq_sb_r,), writes=(rows_r,))
        ge("vector", "tensor_copy", out=logit_sb[:, gsl], in_=pbX[0:E, :],
           reads=(prX,), writes=(logit_r,))
        ge("tensor", "matmul", out=pbY[0:E, :], lhsT=ones_t[0:1, 0:E], rhs=mrow,
           start=True, stop=True, reads=(rows_r, ones_r), writes=(prY,))
        ge("vector", "tensor_scalar", out=sq_sb[0:E, :], in0=pbY[0:E, :],
           scalar1=wgs_t[:], scalar2=None, op0=ALU.mult,
           reads=(prY, wgs_r), writes=(sq_sb_r,))
        ge("vector", "tensor_tensor", out=logit_sb[:, gsl], in0=logit_sb[:, gsl],
           in1=sq_sb[0:E, :], op=ALU.subtract, reads=(logit_r, sq_sb_r), writes=(logit_r,))
        ge("tensor", "matmul", out=pbY[0:E, :], lhsT=ones_t[0:1, 0:E], rhs=rrow,
           start=True, stop=True, reads=(rows_r, ones_r), writes=(prY,))
        ge("vector", "tensor_tensor", out=logit_sb[:, gsl], in0=logit_sb[:, gsl],
           in1=pbY[0:E, :], op=ALU.mult, reads=(logit_r, prY), writes=(logit_r,))
        ge("tensor", "matmul", out=pbX[:], lhsT=ones_t[0:1, :], rhs=mrow,
           start=True, stop=True, reads=(rows_r, ones_r), writes=(prX,), mark=False)
        ge("tensor", "matmul", out=pbY[:], lhsT=ones_t[0:1, :], rhs=rrow,
           start=True, stop=True, reads=(rows_r, ones_r), writes=(prY,))
        for oc in range(NCHUNK):
            ge("vector", "tensor_tensor", out=sq_sb[:], in0=S_t[:, oc, gsl],
               in1=pbX[:], op=ALU.subtract, reads=(S_r, prX),
               writes=(sq_sb_r,), mark=False)
            ge("vector", "tensor_tensor", out=S_t[:, oc, gsl], in0=sq_sb[:],
               in1=pbY[:], op=ALU.mult, reads=(sq_sb_r, prY),
               writes=(S_r,), mark=(oc == NCHUNK - 1))

    if debug:
        ge("sync", "dma_start", out=dbg["d_S1"][:], in_=S_t[:], reads=(S_r,), sem=s_in, inc=16)
        ge("sync", "dma_start", out=dbg["d_logit"][:], in_=logit_sb[:], reads=(logit_r,), sem=s_in, inc=16)
    ag_in_res = R("ag_in")
    ge("gpsimd", "dma_start", out=ag_in[:], in_=logit_sb[:], reads=(logit_r,),
       writes=(ag_in_res,), sem=s_w, inc=16)

    scopeA.close()

    scopeB = ExitStack()
    EB = scopeB.enter_context
    yacc = EB(nc.sbuf_tensor("yacc", [128, 16, D], F32)); yacc_r = R("yacc")
    scrP = EB(nc.sbuf_tensor("scrP", [128, E * 128], F32)); scrP_r = R("scrP")
    statc2 = [EB(nc.sbuf_tensor(f"statc{i}", [128, 6], F32)) for i in range(2)]
    statc2_r = [R(f"statc{i}") for i in range(2)]
    scopeR = ExitStack()
    ER = scopeR.enter_context
    Lp = ER(nc.sbuf_tensor("Lp", [128, E, 128], F32)); Lp_r = R("Lp")
    pp = ER(nc.sbuf_tensor("pp", [128, E, 128], F32)); pp_r = R("pp")
    mx_t = ER(nc.sbuf_tensor("mx", [128, 128], F32)); mx_r = R("mx")
    scrA = ER(nc.sbuf_tensor("scrA", [128, 128], F32)); scrA_r = R("scrA")
    scrB = ER(nc.sbuf_tensor("scrB", [128, 128], F32)); scrB_r = R("scrB")
    best_t = ER(nc.sbuf_tensor("best", [128, 128], F32)); best_r = R("best")
    bestE_t = ER(nc.sbuf_tensor("bestE", [128, 128], F32)); bestE_r = R("bestE")
    ownB = ER(nc.sbuf_tensor("ownB", [128, 16], F32)); ownB_r = R("ownB")
    match_t = ER(nc.sbuf_tensor("match", [128, 16], F32)); match_r = R("match")
    scn_t = ER(nc.sbuf_tensor("scn", [128, 16], F32)); scn_r = R("scn")
    zz16 = ER(nc.sbuf_tensor("zz16", [128, 16], F32)); zz16_r = R("zz16")
    pref_t = ER(nc.sbuf_tensor("pref", [128, 1], F32)); pref_r = R("pref")
    slot_f = ER(nc.sbuf_tensor("slot_f", [128, 16], F32)); slot_fr = R("slot_f")
    lo_t = ER(nc.sbuf_tensor("lo", [128, E], F32)); lo_r = R("lo")
    hi_t = ER(nc.sbuf_tensor("hi", [128, E], F32)); hi_r = R("hi")
    mid_t = ER(nc.sbuf_tensor("mid", [128, E], F32)); mid_r = R("mid")
    part_t = ER(nc.sbuf_tensor("part", [128, E], F32)); part_r = R("part")
    ge_t = ER(nc.sbuf_tensor("geT", [128, E], F32)); geT_r = R("geT")
    sA8 = ER(nc.sbuf_tensor("sA8", [128, E], F32)); sA8_r = R("sA8")
    sB8 = ER(nc.sbuf_tensor("sB8", [128, E], F32)); sB8_r = R("sB8")
    sq2, sq2_r = scrP, scrP_r
    ag_out_r = R("ag_out")

    for tc in range(16):
        j = tc % 3
        for ci in range(NCHUNK):
            ge("tensor", "transpose", out=ps2[j][:, ci * 128:(ci + 1) * 128],
               in_=S_t[:, ci, tc * 128:(tc + 1) * 128], identity=idf_t[:],
               reads=(S_r, idf_r), writes=(pr2[j],), mark=(ci == NCHUNK - 1))
        ge("vector", "tensor_copy", out=yacc[:, tc, :], in_=ps2[j][:],
           reads=(pr2[j],), writes=(yacc_r,))

    sguard = Res("sguard")
    sguard.prod = (s_pe, s_pe.v)

    ge("gpsimd", "collective_compute", "AllGather", ALU.bypass,
       replica_groups=[list(range(NCORES_AG))], ins=[ag_in[:]], outs=[ag_out[:]],
       sem=s_cc, inc=1, reads=(ag_in_res,), writes=(ag_out_r,))
    ag2_in_r = R("ag2_in")
    ge("gpsimd", "dma_start", out=ag2_in[:], in_=ones_t[0:1, 0:16],
       reads=(ones_r,), writes=(ag2_in_r,), sem=s_w, inc=16)
    ge("gpsimd", "collective_compute", "AllGather", ALU.bypass,
       replica_groups=[list(range(NCORES_AG))], ins=[ag2_in[:]], outs=[ag2_out[:]],
       sem=s_cc, inc=1, reads=(ag2_in_r,), writes=(ag_out_r,))
    for e in range(E):
        src = bass.AP(ag_out, e * TC, [[E * TC, NCORES_AG], [128, 16], [1, 128]])
        ge("gpsimd", "dma_start", out=Lp[:, e, :], in_=src, reads=(ag_out_r,),
           writes=(Lp_r,), sem=s_w, inc=16)
    if debug:
        ge("gpsimd", "dma_start", out=dbg["d_ag2"][:], in_=ag_out[:, :],
           reads=(ag_out_r,), sem=s_w, inc=16)
    ge("vector", "tensor_copy", out=mx_t[:], in_=Lp[:, 0, :], reads=(Lp_r,), writes=(mx_r,), mark=False)
    for e in range(1, E):
        ge("vector", "tensor_tensor", out=mx_t[:], in0=mx_t[:], in1=Lp[:, e, :], op=ALU.max,
           reads=(Lp_r, mx_r), writes=(mx_r,), mark=(e == E - 1))
    for e in range(E):
        ge("vector", "tensor_tensor", out=pp[:, e, :], in0=Lp[:, e, :], in1=mx_t[:],
           op=ALU.subtract, reads=(Lp_r, mx_r), writes=(pp_r,))
        ge("scalar", "activation", out=pp[:, e, :], in_=pp[:, e, :], func=AF.Exp,
           reads=(pp_r,), writes=(pp_r,), mark=(e == E - 1))
    ge("vector", "tensor_copy", out=mx_t[:], in_=pp[:, 0, :], reads=(pp_r,), writes=(mx_r,), mark=False)
    for e in range(1, E):
        ge("vector", "tensor_tensor", out=mx_t[:], in0=mx_t[:], in1=pp[:, e, :], op=ALU.add,
           reads=(pp_r, mx_r), writes=(mx_r,), mark=(e == E - 1))
    ge("scalar", "activation", out=scrB[:], in_=mx_t[:], func=AF.Ln,
       reads=(mx_r,), writes=(scrB_r,), mark=False)
    ge("scalar", "activation", out=mx_t[:], in_=scrB[:], func=AF.Exp, scale=-1.0,
       reads=(scrB_r,), writes=(mx_r,))
    for e in range(E):
        ge("vector", "tensor_tensor", out=pp[:, e, :], in0=pp[:, e, :], in1=mx_t[:],
           op=ALU.mult, reads=(pp_r, mx_r), writes=(pp_r,), mark=(e == E - 1))
    ge("vector", "memset", lo_t[:], 0.0, writes=(lo_r,), mark=False)
    ge("vector", "memset", hi_t[:], 1.0, writes=(hi_r,), mark=False)
    ge("vector", "memset", mid_t[:], 0.5, writes=(mid_r,))
    for it in range(NITER):
        for e in range(E):
            ge("vector", "tensor_scalar", out=scrP[:, e * 128:(e + 1) * 128],
               in0=pp[:, e, :], scalar1=mid_t[:, e:e + 1], scalar2=None, op0=ALU.is_ge,
               op1=ALU.add, accum_out=part_t[:, e:e + 1],
               reads=(pp_r, mid_r), writes=(scrP_r, part_r), mark=(e == E - 1))
        ge("tensor", "matmul", out=pbX[:, 0:E], lhsT=ones_t[:, :], rhs=part_t[:],
           start=True, stop=True, reads=(part_r, ones_r), writes=(prX,))
        ge("vector", "tensor_single_scalar", out=ge_t[:], in_=pbX[:, 0:E],
           scalar=float(CAP), op=ALU.is_ge, reads=(prX,), writes=(geT_r,), mark=False)
        ge("vector", "tensor_tensor", out=sA8[:], in0=mid_t[:], in1=lo_t[:],
           op=ALU.subtract, reads=(mid_r, lo_r), writes=(sA8_r,), mark=False)
        ge("vector", "tensor_tensor", out=sB8[:], in0=hi_t[:], in1=mid_t[:],
           op=ALU.subtract, reads=(hi_r, mid_r), writes=(sB8_r,), mark=False)
        ge("vector", "tensor_tensor", out=sA8[:], in0=ge_t[:], in1=sA8[:],
           op=ALU.mult, reads=(geT_r, sA8_r), writes=(sA8_r,), mark=False)
        ge("vector", "tensor_tensor", out=sB8[:], in0=ge_t[:], in1=sB8[:],
           op=ALU.mult, reads=(geT_r, sB8_r), writes=(sB8_r,), mark=False)
        ge("vector", "tensor_tensor", out=lo_t[:], in0=lo_t[:], in1=sA8[:],
           op=ALU.add, reads=(lo_r, sA8_r), writes=(lo_r,), mark=False)
        ge("vector", "tensor_tensor", out=hi_t[:], in0=mid_t[:], in1=sB8[:],
           op=ALU.add, reads=(mid_r, sB8_r), writes=(hi_r,), mark=False)
        ge("vector", "tensor_tensor", out=sA8[:], in0=lo_t[:], in1=hi_t[:],
           op=ALU.add, reads=(lo_r, hi_r), writes=(sA8_r,), mark=False)
        ge("vector", "tensor_single_scalar", out=mid_t[:], in_=sA8[:], scalar=0.5,
           op=ALU.mult, reads=(sA8_r,), writes=(mid_r,), mark=False)
    for e in range(E):
        ge("vector", "tensor_single_scalar", out=scrA[:], in_=pp[:, e, :],
           scalar=lo_t[:, e:e + 1], op=ALU.is_ge,
           reads=(pp_r, lo_r), writes=(scrA_r,), mark=False)
        ge("vector", "tensor_tensor", out=scrA[:], in0=pp[:, e, :], in1=scrA[:],
           op=ALU.add, reads=(pp_r, scrA_r), writes=(scrA_r,), mark=False)
        if e == 0:
            ge("vector", "tensor_copy", out=best_t[:], in_=scrA[:],
               reads=(scrA_r,), writes=(best_r,), mark=False)
            ge("vector", "memset", bestE_t[:], 0.0, writes=(bestE_r,), mark=False)
        else:
            ge("vector", "tensor_tensor", out=scrB[:], in0=scrA[:], in1=best_t[:],
               op=ALU.is_gt, reads=(scrA_r, best_r), writes=(scrB_r,), mark=False)
            ge("vector", "tensor_tensor", out=scrA[:], in0=scrA[:], in1=best_t[:],
               op=ALU.subtract, reads=(scrA_r, best_r), writes=(scrA_r,), mark=False)
            ge("vector", "tensor_tensor", out=scrA[:], in0=scrB[:], in1=scrA[:],
               op=ALU.mult, reads=(scrB_r, scrA_r), writes=(scrA_r,), mark=False)
            ge("vector", "tensor_tensor", out=best_t[:], in0=best_t[:], in1=scrA[:],
               op=ALU.add, reads=(best_r, scrA_r), writes=(best_r,), mark=False)
            ge("vector", "tensor_scalar", out=scrA[:], in0=bestE_t[:], scalar1=float(e),
               scalar2=-1.0, op0=ALU.subtract, op1=ALU.mult,
               reads=(bestE_r,), writes=(scrA_r,), mark=False)
            ge("vector", "tensor_tensor", out=scrA[:], in0=scrB[:], in1=scrA[:],
               op=ALU.mult, reads=(scrB_r, scrA_r), writes=(scrA_r,), mark=False)
            ge("vector", "tensor_tensor", out=bestE_t[:], in0=bestE_t[:], in1=scrA[:],
               op=ALU.add, reads=(bestE_r, scrA_r), writes=(bestE_r,), mark=False)
    for e in range(E):
        ge("vector", "tensor_single_scalar", out=scrA[:], in_=bestE_t[:],
           scalar=float(e), op=ALU.is_equal, reads=(bestE_r,), writes=(scrA_r,), mark=False)
        ge("vector", "tensor_tensor", out=Lp[:, e, :], in0=scrA[:], in1=pp[:, e, :],
           op=ALU.mult, reads=(scrA_r, pp_r), writes=(Lp_r,), mark=(e == E - 1))
    if debug:
        ge("sync", "dma_start", out=dbg["d_pp"][:], in_=pp[:], reads=(pp_r,), sem=s_in, inc=16)
        ge("sync", "dma_start", out=dbg["d_lo"][:], in_=lo_t[:], reads=(lo_r,), sem=s_in, inc=16)
        ge("sync", "dma_start", out=dbg["d_gate"][:], in_=Lp[:], reads=(Lp_r,), sem=s_in, inc=16)
    for e in range(E):
        ge("tensor", "matmul", out=pbY[:, 0:128], lhsT=Lp[:, e, :], rhs=selm_t[:, e, :],
           start=(e == 0), stop=(e == E - 1), reads=(Lp_r, selm_r), writes=(prY,),
           mark=(e == E - 1))
    ge("vector", "tensor_copy", out=owng_t[:], in_=pbY[:, 0:128],
       reads=(prY,), writes=(owng_r,))

    if debug:
        ge("sync", "dma_start", out=dbg["d_own"][:], in_=owng_t[:], reads=(owng_r,), sem=s_in, inc=16)
        ge("sync", "dma_start", out=dbg["d_yacc"][:], in_=yacc[:], reads=(yacc_r,), sem=s_in, inc=16)

    ge("tensor", "matmul", out=pbX[:, 0:128], lhsT=bestE_t[:], rhs=selm_t[:, 0, :],
       start=True, stop=True, reads=(bestE_r, selm_r), writes=(prX,))
    ge("vector", "tensor_copy", out=ownB[:], in_=pbX[:, 0:16], reads=(prX,), writes=(ownB_r,))
    ge("vector", "tensor_copy", out=gate_tm[:], in_=owng_t[:, 0:16],
       reads=(owng_r,), writes=(gate_tm_r,), mark=False)
    for e in range(1, E):
        ge("vector", "tensor_tensor", out=gate_tm[:], in0=gate_tm[:],
           in1=owng_t[:, e * 16:(e + 1) * 16], op=ALU.add,
           reads=(owng_r, gate_tm_r), writes=(gate_tm_r,), mark=False)
    ge("vector", "memset", zz16[:], 0.0, writes=(zz16_r,), mark=False)
    ge("vector", "memset", slot_f[:], 0.0, writes=(slot_fr,))
    for e in range(E):
        ge("vector", "tensor_single_scalar", out=match_t[:], in_=ownB[:],
           scalar=float(e), op=ALU.is_equal, reads=(ownB_r,), writes=(match_r,))
        ge("vector", "tensor_tensor_scan", out=scn_t[:], data0=match_t[:],
           data1=zz16[:], initial=0.0, op0=ALU.add, op1=ALU.add,
           reads=(match_r, zz16_r), writes=(scn_r,))
        ge("tensor", "matmul", out=pbY[:, 0:1], lhsT=lmask_t[:], rhs=scn_t[:, 15:16],
           start=True, stop=True, reads=(lmask_r, scn_r), writes=(prY,))
        ge("vector", "tensor_copy", out=pref_t[:], in_=pbY[:, 0:1],
           reads=(prY,), writes=(pref_r,))
        ge("vector", "tensor_tensor", out=scn_t[:], in0=scn_t[:], in1=match_t[:],
           op=ALU.subtract, reads=(scn_r, match_r), writes=(scn_r,))
        ge("vector", "tensor_scalar", out=scn_t[:], in0=scn_t[:], scalar1=pref_t[:],
           scalar2=float(e * CAPC), op0=ALU.add, op1=ALU.add,
           reads=(scn_r, pref_r), writes=(scn_r,))
        ge("vector", "tensor_single_scalar", out=scn_t[:], in_=scn_t[:],
           scalar=float(e * CAPC + CAPC - 1), op=ALU.min,
           reads=(scn_r,), writes=(scn_r,))
        ge("vector", "tensor_tensor", out=scn_t[:], in0=scn_t[:], in1=match_t[:],
           op=ALU.mult, reads=(scn_r, match_r), writes=(scn_r,))
        ge("vector", "tensor_tensor", out=slot_f[:], in0=slot_f[:], in1=scn_t[:],
           op=ALU.add, reads=(slot_fr, scn_r), writes=(slot_fr,))
    ge("vector", "tensor_copy", out=slot_i[:], in_=slot_f[:],
       reads=(slot_fr,), writes=(slot_r,))
    xsort_r = R("xsort"); ysort_r = R("ysort")
    for tc in range(16):
        ge("gpsimd", "indirect_dma_start", out=xsort[0:128, :],
           out_offset=IndirectOffsetOnAxis(slot_i[:, tc:tc + 1], 0),
           in_=yacc[:, tc, :], in_offset=None,
           reads=(yacc_r, slot_r), writes=(xsort_r,), sem=s_xe, inc=16)

    scopeR.close()
    stackS.close()
    w1b = EB(nc.sbuf_tensor("w1b", [128, NCHUNK, FF], BF16)); w1b_r = R("w1b")
    w2b = EB(nc.sbuf_tensor("w2b", [128, FF // 128, D], BF16)); w2b_r = R("w2b")
    h16 = EB(nc.sbuf_tensor("h16", [128, FF // 128, 512], BF16)); h16_r = R("h16")

    xet = [EB(nc.sbuf_tensor(f"xet{i}", [128, D], F32)) for i in range(4)]
    xet_r = [R(f"xet{i}") for i in range(4)]
    xe_fm = EB(nc.sbuf_tensor("xe_fm", [128, NCHUNK, CAPC], BF16)); xe_fm_r = R("xe_fm")
    ysb = EB(nc.sbuf_tensor("ysb", [128, CAPC // 128, D], F32)); ysb_r = R("ysb")
    ytok = [EB(nc.sbuf_tensor(f"ytok{i}", [128, D], F32)) for i in range(2)]
    ytok_r = [R(f"ytok{i}") for i in range(2)]
    NT = CAPC // 128
    for e in range(E):
        for tl in range(NT):
            ge("gpsimd", "dma_start", out=xet[tl][:],
               in_=xsort[e * CAPC + tl * 128: e * CAPC + (tl + 1) * 128, :],
               reads=(xsort_r,), writes=(xet_r[tl],), sem=s_xe, inc=16)
        ge("gpsimd", "dma_start", out=w1b[:],
           in_=w1[e, :, :].rearrange("(c p) f -> p c f", p=128),
           reads=(sguard,), writes=(w1b_r,), sem=s_w1, inc=16)
        ge("gpsimd", "dma_start", out=w2b[:],
           in_=w2[e, :, :].rearrange("(c p) d -> p c d", p=128),
           reads=(sguard,), writes=(w2b_r,), sem=s_w2, inc=16)
        for tl in range(NT):
            cur = tl
            j = tl % 3
            for ci in range(NCHUNK):
                ge("tensor", "transpose", out=ps2[j][:, ci * 128:(ci + 1) * 128],
                   in_=xet[cur][:, ci * 128:(ci + 1) * 128], identity=idf_t[:],
                   reads=(xet_r[cur], idf_r), writes=(pr2[j],), mark=(ci == NCHUNK - 1))
            ge("vector", "tensor_copy", out=xe_fm[:, :, tl * 128:(tl + 1) * 128],
               in_=ps2[j][:].rearrange("p (c t) -> p c t", c=NCHUNK),
               reads=(pr2[j],), writes=(xe_fm_r,))
        for hcp in range(FF // 256):
            j = hcp % 3
            for hh2 in range(2):
                hc = 2 * hcp + hh2
                dst = ps2[j][:, hh2 * 512:(hh2 + 1) * 512]
                for ci in range(NCHUNK):
                    ge("tensor", "matmul", out=dst,
                       lhsT=w1b[:, ci, hc * 128:(hc + 1) * 128], rhs=xe_fm[:, ci, :],
                       start=(ci == 0), stop=(ci == NCHUNK - 1),
                       reads=(w1b_r, xe_fm_r), writes=(pr2[j],),
                       mark=(ci == NCHUNK - 1 and hh2 == 1))
            ge("scalar", "activation", out=h16[:, 2 * hcp:2 * hcp + 2, :],
               in_=ps2[j][:].rearrange("p (a b) -> p a b", a=2), func=AF.Gelu,
               reads=(pr2[j],), writes=(h16_r,))
        for tc4 in range(NT):
            j = tc4 % 3
            for hc in range(FF // 128):
                ge("tensor", "matmul", out=ps2[j][:, 0:512],
                   lhsT=h16[:, hc, tc4 * 128:(tc4 + 1) * 128],
                   rhs=w2b[:, hc, 0:512],
                   start=(hc == 0), stop=(hc == FF // 128 - 1),
                   reads=(h16_r, w2b_r), writes=(pr2[j],), mark=False)
                ge("tensor", "matmul", out=ps2[j][:, 512:1024],
                   lhsT=h16[:, hc, tc4 * 128:(tc4 + 1) * 128],
                   rhs=w2b[:, hc, 512:1024],
                   start=(hc == 0), stop=(hc == FF // 128 - 1),
                   reads=(h16_r, w2b_r), writes=(pr2[j],),
                   mark=(hc == FF // 128 - 1))
            ge("vector", "tensor_copy", out=ysb[:, tc4, :], in_=ps2[j][:],
               reads=(pr2[j],), writes=(ysb_r,))
        ge("sync", "dma_start",
           out=ysort[e * CAPC:(e + 1) * CAPC, :].rearrange("(tl p) d -> p tl d", p=128),
           in_=ysb[:], reads=(ysb_r,), writes=(ysort_r,), sem=s_ys, inc=16)

    for tc in range(16):
        cur = tc % 2
        statc, statc_r = statc2[cur], statc2_r[cur]
        ge("gpsimd", "indirect_dma_start", out=ytok[cur][:], out_offset=None,
           in_=ysort[0:128, :], in_offset=IndirectOffsetOnAxis(slot_i[:, tc:tc + 1], 0),
           reads=(ysort_r, slot_r), writes=(ytok_r[cur],), sem=s_w, inc=16)
        ge("vector", "scalar_tensor_tensor", out=yacc[:, tc, :], in0=ytok[cur][:],
           scalar=gate_tm[:, tc:tc + 1], in1=yacc[:, tc, :], op0=ALU.mult, op1=ALU.add,
           reads=(ytok_r[cur], gate_tm_r, yacc_r), writes=(yacc_r,))
        ge("scalar", "activation", out=sq2[:], in_=yacc[:, tc, :], func=AF.Identity,
           accum_out=statc[:, 0:1], reads=(yacc_r,), writes=(sq2_r, statc_r), mark=False)
        ge("scalar", "activation", out=sq2[:], in_=yacc[:, tc, :], func=AF.Square,
           accum_out=statc[:, 1:2], reads=(yacc_r,), writes=(sq2_r, statc_r))
        ge("vector", "tensor_single_scalar", out=statc[:, 2:3], in_=statc[:, 0:1],
           scalar=1.0 / D, op=ALU.mult, reads=(statc_r,), writes=(statc_r,), mark=False)
        ge("vector", "tensor_single_scalar", out=statc[:, 3:4], in_=statc[:, 1:2],
           scalar=1.0 / D, op=ALU.mult, reads=(statc_r,), writes=(statc_r,), mark=False)
        ge("vector", "tensor_tensor", out=statc[:, 4:5], in0=statc[:, 2:3],
           in1=statc[:, 2:3], op=ALU.mult, reads=(statc_r,), writes=(statc_r,), mark=False)
        ge("vector", "tensor_tensor", out=statc[:, 4:5], in0=statc[:, 3:4],
           in1=statc[:, 4:5], op=ALU.subtract, reads=(statc_r,), writes=(statc_r,))
        ge("scalar", "activation", out=statc[:, 4:5], in_=statc[:, 4:5], func=AF.Ln,
           bias=eps_t[:], reads=(statc_r, eps_r), writes=(statc_r,), mark=False)
        ge("scalar", "activation", out=statc[:, 5:6], in_=statc[:, 4:5], func=AF.Exp,
           scale=-0.5, reads=(statc_r,), writes=(statc_r,))
        ge("vector", "tensor_scalar", out=yacc[:, tc, :], in0=yacc[:, tc, :],
           scalar1=statc[:, 2:3], scalar2=statc[:, 5:6], op0=ALU.subtract, op1=ALU.mult,
           reads=(yacc_r, statc_r), writes=(yacc_r,))
        ge("sync", "dma_start", out=yout[tc * 128:(tc + 1) * 128, :], in_=yacc[:, tc, :],
           reads=(yacc_r,), sem=s_in, inc=16)

    P.barrier_from("sync", s_in, s_in.v)
    scopeB.close()

    with nc.Block() as block:
        P.play(block)
    ctx.close()
    return nc

NCORES_AG = 8


def kernel(L0_attn_hidden_states, L0_attn_Wq, L0_attn_bq, L0_attn_Wk,
           L0_attn_Wv, L0_attn_bv, L0_attn_Wo, L0_attn_bo,
           L0_ffn_Wg, L0_ffn_W1, L0_ffn_W2):
    x = np.ascontiguousarray(np.asarray(L0_attn_hidden_states, dtype=np.float32))
    wqT = np.ascontiguousarray(np.asarray(L0_attn_Wq, np.float32).T)
    wkT = np.ascontiguousarray(np.asarray(L0_attn_Wk, np.float32).T)
    wvT = np.ascontiguousarray(np.asarray(L0_attn_Wv, np.float32).T)
    woT = np.ascontiguousarray(np.asarray(L0_attn_Wo, np.float32).T)
    bq2 = np.ascontiguousarray(np.asarray(L0_attn_bq, np.float32).reshape(NCHUNK, 128).T)
    bo2 = np.ascontiguousarray(np.asarray(L0_attn_bo, np.float32).reshape(NCHUNK, 128).T)
    bvr = np.ascontiguousarray(np.asarray(L0_attn_bv, np.float32).reshape(1, HD))
    wg = np.ascontiguousarray(np.asarray(L0_ffn_Wg, np.float32))
    w1 = np.ascontiguousarray(np.asarray(L0_ffn_W1, np.float32))
    w2 = np.ascontiguousarray(np.asarray(L0_ffn_W2, np.float32))
    identf = np.eye(128, dtype=np.float32)
    onesf = np.ones((128, 128), dtype=np.float32)
    lmaskf = np.ascontiguousarray(np.triu(np.ones((128, 128), dtype=np.float32), 1))

    nc = build_nc()

    in_maps = []
    for c in range(8):
        b, hh = c // 2, c % 2
        xbat = x[b]
        if hh == 0:
            xb_c = xbat
        else:
            xb_c = np.ascontiguousarray(np.concatenate([xbat[2048:], xbat[:2048]], axis=0))
        selmat = np.zeros((E, 128, 128), dtype=np.float32)
        for e in range(E):
            for j in range(16):
                selmat[e, c * 16 + j, e * 16 + j] = 1.0
        in_maps.append({
            "xb": xb_c, "wqT": wqT, "wkT": wkT, "wvT": wvT, "woT": woT,
            "bq2": bq2, "bo2": bo2, "bvr": bvr, "wg": wg, "w1": w1, "w2": w2,
            "identf": identf, "onesf": onesf, "selmat": selmat, "lmask": lmaskf,
        })

    res = run_bass_kernel_spmd(nc, in_maps, core_ids=list(range(8)))
    out = np.empty((B, T, D), dtype=np.float32)
    for c in range(8):
        b, hh = c // 2, c % 2
        out[b, hh * 2048:(hh + 1) * 2048] = res.results[c]["yout"]
    return out
